# version 29
# baseline (speedup 1.0000x reference)
from collections import defaultdict
from contextlib import ExitStack

import numpy as np
import concourse.bass as bass
import concourse.mybir as mybir
from concourse.bass_utils import run_bass_kernel_spmd

F32 = mybir.dt.float32
BF16 = mybir.dt.bfloat16
AF = mybir.ActivationFunctionType
ALU = mybir.AluOpType
AX = mybir.AxisListType

P = 128
CC = 8
TT = 256
NSEQ = 2
NTS = 8
NT = NSEQ * NTS
C = 1024
EPS = 1e-5
GEPS = 1e-6


class Bld:

    NDMA = 8

    def __init__(self):
        self.mode = "record"
        self.cnt = defaultdict(int)
        self.ev = {}
        self.eng = {}
        self.sem = {}
        self.lastw = defaultdict(int)
        self.dma_cnt = 0

    def start_emit(self):
        self.mode = "emit"
        self.cnt = defaultdict(int)
        self.lastw = defaultdict(int)
        self.dma_cnt = 0

    def op(self, en, meth, *a, ev=None, inc=1, dma=False, **kw):
        if dma:
            key = f"dma{self.dma_cnt % self.NDMA}"
            self.dma_cnt += 1
            inc = 16
        else:
            key = en
        if self.mode == "record":
            if ev is not None:
                self.cnt[key] += inc
                self.ev[ev] = (key, self.cnt[key])
            return None
        if dma and self.cnt[key] >= 16:
            self.wge(en, key, self.cnt[key])
        instr = getattr(self.eng[en], meth)(*a, **kw)
        if ev is not None:
            self.cnt[key] += inc
            assert self.ev[ev] == (key, self.cnt[key]), (ev, self.ev[ev], key, self.cnt[key])
            instr.then_inc(self.sem[key], inc)
        return instr

    def w(self, en, ev, force=False):
        if self.mode == "record":
            assert isinstance(ev, str)
            return
        pen, val = self.ev[ev]
        if pen == en and not force:
            return
        if self.lastw[(en, pen)] >= val:
            return
        self.eng[en].wait_ge(self.sem[pen], val)
        self.lastw[(en, pen)] = val

    def ws(self, en, ev):
        self.w(en, ev, force=True)

    def wge(self, en, pen, val):
        if self.mode == "record":
            return
        if pen == en or self.lastw[(en, pen)] >= val:
            return
        self.eng[en].wait_ge(self.sem[pen], val)
        self.lastw[(en, pen)] = val

    def final_wait(self, en, ev):
        if self.mode == "record":
            return
        pen, val = self.ev[ev]
        self.eng[en].wait_ge(self.sem[pen], val)


def build_nc():
    nc = bass.Bass("TRN2", target_bir_lowering=False, debug=False)

    d_x = nc.dram_tensor("xt", [NSEQ, C, 2048], F32, kind="ExternalInput").ap()
    d_out = nc.dram_tensor("out_t", [NSEQ, C, 2048], F32, kind="ExternalOutput").ap()
    d_wk = nc.dram_tensor("wkT", [C, C], F32, kind="ExternalInput").ap()
    d_wv = nc.dram_tensor("wvT", [C, C], F32, kind="ExternalInput").ap()
    d_wr = nc.dram_tensor("wrT", [C, C], F32, kind="ExternalInput").ap()
    d_wo = nc.dram_tensor("woT", [C, C], F32, kind="ExternalInput").ap()
    d_wf = nc.dram_tensor("wfT", [C, C], F32, kind="ExternalInput").ap()
    d_at = nc.dram_tensor("aT", [C, 128], F32, kind="ExternalInput").ap()
    d_bt = nc.dram_tensor("bT", [128, C], F32, kind="ExternalInput").ap()
    d_rt = nc.dram_tensor("rT", [C, 8], F32, kind="ExternalInput").ap()
    d_tmk = nc.dram_tensor("tmk", [P, CC], F32, kind="ExternalInput").ap()
    d_tmv = nc.dram_tensor("tmv", [P, CC], F32, kind="ExternalInput").ap()
    d_tmr = nc.dram_tensor("tmr", [P, CC], F32, kind="ExternalInput").ap()
    d_td = nc.dram_tensor("td", [P, CC], F32, kind="ExternalInput").ap()
    d_tf = nc.dram_tensor("tf", [P, CC], F32, kind="ExternalInput").ap()
    d_bias = nc.dram_tensor("bias", [P, CC], F32, kind="ExternalInput").ap()
    d_ident = nc.dram_tensor("ident", [P, P], F32, kind="ExternalInput").ap()
    d_eexp = nc.dram_tensor("eexp", [8, 128], F32, kind="ExternalInput").ap()

    with ExitStack() as stack:
        def sb(name, shape, dtype=F32):
            return stack.enter_context(nc.sbuf_tensor(name, shape, dtype))

        def psum(name, shape):
            return stack.enter_context(nc.psum_tensor(name, shape, F32))

        wk = sb("wk", [P, CC, C], BF16)
        wv = sb("wv", [P, CC, C], BF16)
        wr = sb("wr", [P, CC, C], BF16)
        wo = sb("wo", [P, CC, C], BF16)
        wf = sb("wf", [P, CC, C], BF16)
        at = sb("at", [P, CC, 128], BF16)
        bt = sb("bt", [P, C], BF16)
        rt = sb("rt", [P, CC, 8], BF16)
        ident = sb("identt", [P, P], F32)
        eexp = sb("eexpt", [8, 128], F32)
        tmk1 = sb("tmk1", [P, CC], F32)
        tmv1 = sb("tmv1", [P, CC], F32)
        tmr1 = sb("tmr1", [P, CC], F32)
        ew = sb("ew", [P, CC], F32)
        eu = sb("eu", [P, CC], F32)
        bias = sb("biast", [P, CC], F32)
        c_tmk = sb("c_tmk", [P, CC], F32)
        c_tmv = sb("c_tmv", [P, CC], F32)
        c_tmr = sb("c_tmr", [P, CC], F32)
        c_td = sb("c_td", [P, CC], F32)
        c_tf = sb("c_tf", [P, CC], F32)
        onecol = sb("onecol", [P, 1], BF16)
        onesb = sb("onesb", [P, TT], BF16)
        c_eps = sb("c_eps", [P, 1], F32)
        c_mhalf = sb("c_mhalf", [P, 1], F32)
        c_mone = sb("c_mone", [P, 1], F32)
        ones1 = sb("ones1", [1, P], F32)
        rt_f = sb("rt_f", [P, CC, 8], F32)
        negb = sb("negb", [P, CC], F32)

        pb = sb("pb", [P, CC, TT + 1], F32)
        qb = sb("qb", [P, CC, TT + 1], F32)

        xb = [sb(f"xb{j}", [P, CC, TT], F32) for j in range(2)]
        x_bf = sb("x_bf", [P, CC, TT], BF16)
        xn = [sb(f"xn{j}", [P, CC, TT + 2], BF16) for j in range(2)]
        xsq = sb("xsq", [P, CC, TT], BF16)
        xk = sb("xk", [P, CC, TT], BF16)
        xv = sb("xv", [P, CC, TT], BF16)
        xr = sb("xr", [P, CC, TT], BF16)
        ek = sb("ek", [P, CC, TT], F32)
        ekv = sb("ekv", [P, CC, TT], F32)
        numb = sb("numb", [P, CC, TT], F32)
        denb = sb("denb", [P, CC, TT], F32)
        erb = sb("erb", [P, CC, TT], BF16)
        zb = sb("zb", [P, CC, TT], BF16)
        t2 = ekv
        ob = sb("ob", [P, CC, TT], F32)
        gb = numb
        ghb = sb("ghb", [P, TT], BF16)
        gateb_s = sb("gateb_s", [P, TT], F32)
        gate_s = sb("gate_s", [P, 16], F32)
        lt_s = sb("lt_s", [P, 16], F32)
        gT_s = sb("gT_s", [8, TT], F32)
        rstdb_s = sb("rstdb_s", [P, TT], F32)
        murb_s = sb("murb_s", [P, TT], F32)
        mu2_r = sb("mu2_r", [1, TT], F32)
        var_r = sb("var_r", [1, TT], F32)
        rstd_r = sb("rstd_r", [1, TT], F32)
        mur_r = sb("mur_r", [1, TT], F32)
        m1_s = sb("m1_s", [P, 2], F32)
        m2_s = sb("m2_s", [P, 2], F32)
        graw_s = sb("graw_s", [P, 16], F32)
        msk1_s = sb("msk1_s", [P, 16], F32)
        l2_s = sb("l2_s", [P, 16], F32)
        msk_s = sb("msk_s", [P, 16], F32)
        tg_s = sb("tg_s", [P, 16], F32)
        sall_s = sb("sall_s", [P, 2], F32)
        st_s = sb("st_s", [P, 2], F32)
        den2_s = sb("den2_s", [P, 2], F32)
        rec2_s = sb("rec2_s", [P, 2], F32)

        ps_k = psum("ps_k", [P, 256])
        ps_v = psum("ps_v", [P, 256])
        ps_r = psum("ps_r", [P, 256])
        ps_base0 = psum("ps_base0", [P, 512])
        ps_base1 = psum("ps_base1", [P, 512])
        ps_s = psum("ps_s", [P, 512])
        ps_bc = psum("ps_bc", [P, 512])
        ps_h = psum("ps_h", [P, 512])
        ps_g = ps_bc
        s1_ps = ps_bc[0:1, 0:256]
        s2_ps = ps_bc[0:1, 256:512]

        def ps_base(m):
            return (ps_base0 if m % 2 == 0 else ps_base1)[:, 0:256]

        b = Bld()

        scr = [ekv[:], numb[:], denb[:],
               pb.ap().rearrange("p cc t -> p (cc t)")[:, 0:CC * TT],
               qb.ap().rearrange("p cc t -> p (cc t)")[:, 0:CC * TT]]
        bigW = [(d_wk, wk), (d_wv, wv), (d_wr, wr), (d_wf, wf), (d_wo, wo)]
        NQ = len(bigW) * 4

        ACT_QS = [1, 3, 5, 7, 9, 11]
        DVE_QS = [q for q in range(NQ) if q not in ACT_QS]

        def x_dram(i):
            bq, tq = divmod(i, NTS)
            return d_x[bq].rearrange("(cc p) t -> p cc t", p=P)[:, :, tq * TT:(tq + 1) * TT]

        def o_dram(i):
            bq, tq = divmod(i, NTS)
            return d_out[bq].rearrange("(cc p) t -> p cc t", p=P)[:, :, tq * TT:(tq + 1) * TT]

        def gen_sync(b):
            en = "sync"
            for i in range(2):
                b.op(en, "dma_start", out=xb[i % 2][:], in_=x_dram(i), ev=f"ldx_{i}", dma=True)
            for nm, dst, src in [("tmk", c_tmk, d_tmk), ("tmv", c_tmv, d_tmv),
                                 ("tmr", c_tmr, d_tmr), ("td", c_td, d_td),
                                 ("tf", c_tf, d_tf), ("bias", bias, d_bias),
                                 ("ident", ident, d_ident), ("eexp", eexp, d_eexp)]:
                b.op(en, "dma_start", out=dst[:], in_=src, ev=f"ld_{nm}", dma=True)
            b.op(en, "dma_start", out=rt_f[:],
                 in_=d_rt.rearrange("(cc p) e -> p cc e", p=P), ev="ld_rt", dma=True)
            b.op(en, "dma_start",
                 out=ob[:, :, 0:128],
                 in_=d_at.rearrange("(cc p) e -> p cc e", p=P), ev="ld_A", dma=True)
            b.op(en, "dma_start",
                 out=ob[:, :, 128:256],
                 in_=d_bt.rearrange("p (cc e) -> p cc e", cc=CC), ev="ld_B", dma=True)
            for q in range(NQ):
                m, quarter = divmod(q, 4)
                src, _ = bigW[m]
                if q >= 5:
                    b.w(en, f"cast_{q - 5}")
                b.op(en, "dma_start",
                     out=scr[q % 5],
                     in_=src.rearrange("(cc p) d -> p cc d", p=P)[:, quarter * 2:quarter * 2 + 2, :], ev=f"ldq_{q}", dma=True)
            for i in range(2, NT + 2):
                if i < NT:
                    b.w(en, f"out_{i - 2}_{CC - 1}")
                    b.op(en, "dma_start", out=xb[i % 2][:], in_=x_dram(i), ev=f"ldx_{i}", dma=True)
                j = i - 2
                od = o_dram(j)
                for dd in range(CC):
                    b.w(en, f"out_{j}_{dd}")
                    b.op(en, "dma_start", out=od[:, dd:dd + 1, :], in_=ob[:, dd:dd + 1, :],
                         ev=f"st_{j}_{dd}", dma=True)
            for dd in range(CC):
                b.final_wait(en, f"st_{NT - 1}_{dd}")

        def gen_vector(b):
            en = "vector"
            b.op(en, "memset", onecol[:], 1.0 / 1024.0, ev="c_one")
            b.op(en, "memset", ones1[:], 1.0, ev="c_ones1")
            b.op(en, "memset", c_eps[:], EPS, ev="c_eps")
            b.op(en, "memset", c_mhalf[:], -0.5, ev="c_mhalf")
            b.op(en, "memset", c_mone[:], -1.0, ev="c_mone")
            b.w(en, "ld_rt")
            b.op(en, "tensor_copy", rt[:], rt_f[:], ev="cast_rt")
            for nm, dst, src in [("tmk", tmk1, c_tmk), ("tmv", tmv1, c_tmv), ("tmr", tmr1, c_tmr)]:
                b.w(en, f"ld_{nm}")
                b.op(en, "tensor_scalar_add", dst[:], src[:], -1.0, ev=f"c_{nm}")
            b.w(en, "ld_bias")
            b.op(en, "tensor_scalar_mul", negb[:], bias[:], -1.0, ev="c_negb")

            def weight_casts():
                for q in DVE_QS:
                    m, quarter = divmod(q, 4)
                    _, dst = bigW[m]
                    b.w(en, f"ldq_{q}")
                    b.op(en, "tensor_copy",
                         dst[:, quarter * 2:quarter * 2 + 2, :], scr[q % 5], ev=f"cast_{q}")
                b.w(en, "ld_A")
                b.op(en, "tensor_copy", at[:, :, :], ob[:, :, 0:128], ev="cast_A")
                b.w(en, "ld_B")
                b.op(en, "tensor_copy",
                     bt.ap().rearrange("p (cc e) -> p cc e", cc=CC),
                     ob[:, :, 128:256], ev="cast_B")

            def xbf_op(j):
                b.w(en, f"ldx_{j}")
                if j >= 1:
                    b.w(en, f"s12_{j - 1}")
                    b.w(en, f"xsq_{j - 1}")
                b.op(en, "tensor_copy",
                     x_bf.ap().rearrange("p cc t -> p (cc t)"),
                     xb[j % 2].ap().rearrange("p cc t -> p (cc t)"), ev=f"xbf_{j}")

            for i in range(NT):
                if i == 0:
                    xbf_op(0)
                def t2_ops(dds):
                    j2 = i - 1
                    for dd2 in dds:
                        b.w(en, f"wo_{j2}_{dd2}")
                        b.w(en, f"silu_{j2}_{dd2}")
                        b.ws(en, f"num_{j2}_{dd2}")
                        b.op(en, "tensor_tensor",
                             t2[:, dd2, :], ps_s[:, (dd2 % 2) * 256:(dd2 % 2) * 256 + 256],
                             gb[:, dd2, :], ALU.mult, ev=f"t2_{j2}_{dd2}")
                def var_op(j):
                    b.w(en, f"mu2_{j}")
                    if j >= 1:
                        b.w(en, f"rstd_{j - 1}")
                        b.w(en, f"bc_{j - 1}")
                    b.op(en, "tensor_sub", var_r[:], s2_ps, mu2_r[:], ev=f"var_{j}")

                def mur_op(j):
                    b.w(en, f"rstd_{j}")
                    b.op(en, "tensor_mul", mur_r[:], s1_ps, rstd_r[:], ev=f"mur_{j}")

                if i == 0:
                    var_op(0)
                    mur_op(0)
                b.w(en, f"evm_{i}")
                if i >= 1:
                    b.ws(en, f"den_{i - 1}_{CC - 1}")
                b.op(en, "tensor_tensor",
                     ek.ap(), xb[i % 2].ap(),
                     murb_s.ap().unsqueeze(1).broadcast_to([P, CC, TT]),
                     ALU.subtract, ev=f"d1_{i}")
                if i >= 1:
                    t2_ops([0])
                b.ws(en, f"d1_{i}")
                b.w(en, f"evr_{i}")
                if i >= 2:
                    b.w(en, f"wb_{i - 2}_{CC - 1}")
                b.op(en, "tensor_tensor",
                     xn[i % 2][:, :, 2:2 + TT], ek.ap(),
                     rstdb_s.ap().unsqueeze(1).broadcast_to([P, CC, TT]), ALU.mult, ev=f"xnm_{i}")
                if i % NTS == 0:
                    b.op(en, "memset", xn[i % 2][:, :, 1:2], 0.0, ev=f"xnb_{i}")
                else:
                    b.ws(en, f"xnm_{i - 1}")
                    b.op(en, "tensor_copy",
                         xn[i % 2][:, :, 1:2], xn[(i - 1) % 2][:, :, TT + 1:TT + 2], ev=f"xnb_{i}")
                if i >= 1:
                    t2_ops([1])
                b.w(en, f"s12_{i}")
                b.ws(en, f"xnm_{i}")
                b.ws(en, f"xnb_{i}")
                b.op(en, "tensor_tensor",
                     xsq.ap(), xn[i % 2][:, :, 2:2 + TT], xn[i % 2][:, :, 1:1 + TT],
                     ALU.subtract, ev=f"dmix_{i}")
                if i >= 1:
                    t2_ops([2])
                b.ws(en, f"dmix_{i}")
                if i == 0:
                    b.ws(en, "c_tmk")
                    b.ws(en, "c_tmv")
                    b.ws(en, "c_tmr")
                if i >= 1:
                    b.w(en, f"r_{i - 1}_{CC - 1}")
                for cc in range(CC):
                    b.op(en, "scalar_tensor_tensor",
                         xk[:, cc, :], xsq[:, cc, :], tmk1[:, cc:cc + 1],
                         xn[i % 2][:, cc, 2:2 + TT], ALU.mult, ALU.add,
                         ev=f"mixk_{i}" if cc == CC - 1 else None)
                gate_steps = []

                def gstep(k):
                    if k < len(gate_steps):
                        gate_steps[k]()

                def _g0():
                    b.w(en, f"lts_{i}")
                    if i >= 1:
                        b.w(en, f"tg_{i - 1}")
                    b.op(en, "reduce_max", m1_s[:], lt_s.ap().rearrange("p (c e) -> p c e", c=2),
                         axis=AX.X, ev=f"gm1_{i}")

                def _g1():
                    b.ws(en, f"gm1_{i}")
                    b.op(en, "tensor_tensor", msk1_s.ap().rearrange("p (c e) -> p c e", c=2),
                         lt_s.ap().rearrange("p (c e) -> p c e", c=2),
                         m1_s.ap().unsqueeze(2).broadcast_to([P, 2, 8]), ALU.is_equal, ev=f"gmk1_{i}")

                def _g2():
                    b.ws(en, f"gmk1_{i}")
                    b.op(en, "scalar_tensor_tensor", l2_s[:], msk1_s[:], -1e30, lt_s[:],
                         ALU.mult, ALU.add, ev=f"gl2_{i}")

                def _g3():
                    b.ws(en, f"gl2_{i}")
                    b.op(en, "reduce_max", m2_s[:], l2_s.ap().rearrange("p (c e) -> p c e", c=2),
                         axis=AX.X, ev=f"gm2_{i}")

                def _g4():
                    b.ws(en, f"gm2_{i}")
                    b.op(en, "tensor_tensor", msk_s.ap().rearrange("p (c e) -> p c e", c=2),
                         lt_s.ap().rearrange("p (c e) -> p c e", c=2),
                         m2_s.ap().unsqueeze(2).broadcast_to([P, 2, 8]), ALU.is_ge, ev=f"gmsk_{i}")

                def _g5():
                    b.w(en, f"graw_{i}")
                    b.ws(en, f"gmsk_{i}")
                    b.op(en, "tensor_mul", tg_s[:], graw_s[:], msk_s[:], ev=f"gtg_{i}")

                def _g6():
                    b.op(en, "reduce_sum", sall_s[:], graw_s.ap().rearrange("p (c e) -> p c e", c=2),
                         axis=AX.X, ev=f"gsa_{i}")

                def _g7():
                    b.ws(en, f"gtg_{i}")
                    b.op(en, "reduce_sum", st_s[:], tg_s.ap().rearrange("p (c e) -> p c e", c=2),
                         axis=AX.X, ev=f"gss_{i}")

                def _g8():
                    b.ws(en, f"gsa_{i}")
                    b.ws(en, f"gss_{i}")
                    b.op(en, "scalar_tensor_tensor", den2_s[:], sall_s[:], GEPS, st_s[:],
                         ALU.mult, ALU.add, ev=f"gd2_{i}")

                def _g9():
                    b.ws(en, f"gd2_{i}")
                    b.op(en, "reciprocal", rec2_s[:], den2_s[:], ev=f"grc_{i}")

                def _g10():
                    b.ws(en, f"grc_{i}")
                    b.op(en, "tensor_tensor",
                         gate_s.ap().rearrange("p (c e) -> p c e", c=2),
                         tg_s.ap().rearrange("p (c e) -> p c e", c=2),
                         rec2_s.ap().unsqueeze(2).broadcast_to([P, 2, 8]), ALU.mult, ev=f"gsm_{i}")

                gate_steps = [_g0, _g1, _g2, _g3, _g4, _g5, _g6, _g7, _g8, _g9, _g10]
                gk = 0
                for cc in range(CC):
                    b.op(en, "scalar_tensor_tensor",
                         xv[:, cc, :], xsq[:, cc, :], tmv1[:, cc:cc + 1],
                         xn[i % 2][:, cc, 2:2 + TT], ALU.mult, ALU.add,
                         ev=f"mixv_{i}" if cc == CC - 1 else None)
                    gstep(gk); gk += 1
                for cc in range(CC):
                    b.op(en, "scalar_tensor_tensor",
                         xr[:, cc, :], xsq[:, cc, :], tmr1[:, cc:cc + 1],
                         xn[i % 2][:, cc, 2:2 + TT], ALU.mult, ALU.add,
                         ev=f"mixr_{i}" if cc == CC - 1 else None)
                    gstep(gk); gk += 1
                while gk < len(gate_steps):
                    gstep(gk); gk += 1
                if i == 0:
                    weight_casts()
                if i == 0:
                    b.w(en, "c_ew")
                if i >= 1:
                    b.ws(en, f"tts_{i - 1}_{CC - 1}")
                if i % NTS == 0:
                    b.op(en, "memset", pb[:, :, 0:1], 0.0)
                    b.op(en, "memset", qb[:, :, 0:1], 0.0, ev=f"carry_{i}")
                else:
                    b.op(en, "tensor_copy", pb[:, :, 0:1], pb[:, :, TT:TT + 1])
                    b.op(en, "tensor_copy", qb[:, :, 0:1], qb[:, :, TT:TT + 1], ev=f"carry_{i}")
                b.ws(en, f"carry_{i}")
                t2_pre = {0: [3], 1: [5], 3: [7]}
                t2_post = {0: [4], 1: [6]}
                g_sched = {3: 0, 4: 1, 5: 2, 6: 3, 7: 4}

                def g_op(m):
                    b.w(en, f"sbC_{i}_{m}")
                    b.op(en, "scalar_tensor_tensor",
                         gb[:, m, :], ps_base(m),
                         bias[:, m:m + 1], gb[:, m, :],
                         ALU.add, ALU.mult, ev=f"silu_{i}_{m}")

                for dd in range(CC):
                    if i >= 1 and dd in t2_pre:
                        t2_ops(t2_pre[dd])
                    if i >= 1 and dd <= 2:
                        b.w(en, f"out_{i - 1}_{dd}")
                    b.w(en, f"ek_{i}_{dd}")
                    b.w(en, f"v_{i}_{dd}")
                    if i >= 1:
                        b.ws(en, f"t2_{i - 1}_{dd}")
                        if dd > 2:
                            b.w(en, f"out_{i - 1}_{dd}")
                    b.op(en, "tensor_tensor",
                         ekv[:, dd, :], ek[:, dd, :], ps_v[:], ALU.mult, ev=f"ekv_{i}_{dd}")
                    b.ws(en, f"ekv_{i}_{dd}")
                    b.op(en, "tensor_tensor_scan",
                         pb[:, dd, 1:1 + TT], ew[:, dd:dd + 1].broadcast_to([P, TT]),
                         ekv[:, dd, :], pb[:, dd, 0:1], ALU.mult, ALU.add, ev=f"ttp_{i}_{dd}")
                    b.op(en, "tensor_tensor_scan",
                         qb[:, dd, 1:1 + TT], ew[:, dd:dd + 1].broadcast_to([P, TT]),
                         ek[:, dd, :], qb[:, dd, 0:1], ALU.mult, ALU.add, ev=f"tts_{i}_{dd}")
                    if dd == 1:
                        b.w(en, f"h_{i}")
                        b.w(en, f"gB_{i}")
                        if i >= 1:
                            b.w(en, f"wb_{i - 1}_{CC - 1}")
                        b.op(en, "tensor_tensor", ghb[:], ps_h[:, 0:256], gateb_s[:],
                             ALU.mult, ev=f"gh_{i}")
                    if dd == 0:
                        if i == 0:
                            b.w(en, "c_eu")
                        if i >= 1:
                            b.w(en, f"z_{i - 1}_{CC - 2}")
                    b.ws(en, f"ttp_{i}_{dd}")
                    b.op(en, "scalar_tensor_tensor",
                         numb[:, dd, :], ekv[:, dd, :], eu[:, dd:dd + 1],
                         pb[:, dd, 0:TT], ALU.mult, ALU.add, ev=f"num_{i}_{dd}")
                    b.ws(en, f"tts_{i}_{dd}")
                    b.op(en, "scalar_tensor_tensor",
                         denb[:, dd, :], ek[:, dd, :], eu[:, dd:dd + 1],
                         qb[:, dd, 0:TT], ALU.mult, ALU.add, ev=f"den_{i}_{dd}")
                    if i >= 1 and dd in t2_post:
                        t2_ops(t2_post[dd])
                    if dd in g_sched:
                        g_op(g_sched[dd])
                    if dd == 5 and i + 1 < NT:
                        xbf_op(i + 1)
                    if dd == CC - 1:
                        b.w(en, f"oner_{i}_{CC - 1}")
                        b.ws(en, f"den_{i}_{CC - 1}")
                        b.op(en, "tensor_tensor",
                             ek[:, CC - 1, :], denb[:, CC - 1, :], erb[:, CC - 1, :],
                             ALU.mult, ev=f"denp_{i}_{CC - 1}")
                        b.ws(en, f"denp_{i}_{CC - 1}")
                        b.op(en, "reciprocal",
                             denb[:, CC - 1, :], ek[:, CC - 1, :], ev=f"recd_{i}")
                        b.ws(en, f"recd_{i}")
                        if i >= 1:
                            b.w(en, f"wo_{i - 1}_{CC - 1}")
                        b.op(en, "tensor_tensor",
                             zb[:, CC - 1, :], numb[:, CC - 1, :], denb[:, CC - 1, :],
                             ALU.mult, ev=f"y_{i}")
                g_op(5)
                if i + 1 < NT:
                    var_op(i + 1)
                g_op(6)
                if i + 1 < NT:
                    mur_op(i + 1)
                g_op(7)
            j = NT - 1
            for dd in range(CC):
                b.w(en, f"wo_{j}_{dd}")
                b.w(en, f"silu_{j}_{dd}")
                b.ws(en, f"num_{j}_{dd}")
                b.op(en, "tensor_tensor",
                     t2[:, dd, :], ps_s[:, (dd % 2) * 256:(dd % 2) * 256 + 256],
                     gb[:, dd, :], ALU.mult, ev=f"t2_{j}_{dd}")

        def gen_scalar(b):
            en = "scalar"
            b.w(en, "ld_td")
            b.w(en, "c_mone")
            b.w(en, "c_mhalf")
            b.w(en, "c_eps")
            b.op(en, "activation", ew[:], c_td[:], AF.Exp, ev="ewt")
            b.ws(en, "ewt")
            b.op(en, "activation", ew[:], ew[:], AF.Exp, scale=c_mone[:], ev="c_ew")
            b.w(en, "ld_tf")
            b.op(en, "activation", eu[:], c_tf[:], AF.Exp, ev="c_eu")

            for i in range(NT):
                def xsq_op(j):
                    b.w(en, f"xbf_{j}")
                    if j >= 1:
                        b.w(en, f"mixr_{j - 1}")
                    b.op(en, "activation",
                         xsq.ap().rearrange("p cc t -> p (cc t)"),
                         x_bf.ap().rearrange("p cc t -> p (cc t)"), AF.Square, ev=f"xsq_{j}")

                def mu2_op(j):
                    b.w(en, f"s12_{j}")
                    if j >= 1:
                        b.w(en, f"var_{j - 1}")
                    b.op(en, "activation", mu2_r[:], s1_ps, AF.Square, ev=f"mu2_{j}")

                def ln_op(j):
                    b.w(en, f"var_{j}")
                    if j >= 1:
                        b.w(en, f"bc_{j - 1}")
                    b.op(en, "activation", rstd_r[:], var_r[:], AF.Ln,
                         bias=c_eps[0:1, :], ev=f"ln_{j}")

                def rstd_op(j):
                    b.ws(en, f"ln_{j}")
                    b.op(en, "activation", rstd_r[:], rstd_r[:], AF.Exp,
                         scale=c_mhalf[0:1, :], ev=f"rstd_{j}")

                def ev_ops(j):
                    b.w(en, f"bc_{j}")
                    if j >= 1:
                        b.w(en, f"xnm_{j - 1}")
                        b.w(en, f"d1_{j - 1}")
                    b.op(en, "copy", rstdb_s[:], ps_bc[:, 0:256], ev=f"evr_{j}")
                    b.op(en, "copy", murb_s[:], ps_bc[:, 256:512], ev=f"evm_{j}")

                if i == 0:
                    xsq_op(0)
                    mu2_op(0)
                    ln_op(0)
                    rstd_op(0)
                ev_ops(i)
                b.w(en, f"lt_{i}")
                if i >= 1:
                    b.w(en, f"gsm_{i - 1}")
                b.op(en, "copy", lt_s[:], ps_h[:, 256:272], ev=f"lts_{i}")
                b.ws(en, f"lts_{i}")
                b.op(en, "activation", graw_s[:], lt_s[:], AF.Exp, ev=f"graw_{i}")
                if i == 0:
                    for q in ACT_QS:
                        m0, quarter = divmod(q, 4)
                        _, dst = bigW[m0]
                        b.w(en, f"ldq_{q}")
                        b.op(en, "copy",
                             dst[:, quarter * 2:quarter * 2 + 2, :], scr[q % 5],
                             ev=f"cast_{q}")
                def silu_op(m):
                    half = ps_base(m)
                    b.w(en, f"wb_{i}_{m}")
                    if i == 0 and m == 0:
                        b.w(en, "c_negb")
                    if m == CC - 1:
                        b.w(en, f"y_{i}")
                    else:
                        b.w(en, f"z_{i}_{m}")
                    b.op(en, "activation", gb[:, m, :], half, AF.Exp,
                         scale=c_mone[:], bias=negb[:, m:m + 1], ev=f"sbA_{i}_{m}")
                    b.ws(en, f"sbA_{i}_{m}")
                    b.op(en, "activation", gb[:, m, :], gb[:, m, :], AF.Ln,
                         bias=1.0, ev=f"sbB_{i}_{m}")
                    b.ws(en, f"sbB_{i}_{m}")
                    b.op(en, "activation", gb[:, m, :], gb[:, m, :], AF.Exp,
                         scale=c_mone[:], ev=f"sbC_{i}_{m}")

                def rec_op(cc):
                    b.w(en, f"denp_{i}_{cc}")
                    b.op(en, "activation", denb[:, cc, :], denb[:, cc, :], AF.Ln, ev=f"lnr_{i}_{cc}")
                    b.ws(en, f"lnr_{i}_{cc}")
                    b.op(en, "activation",
                         denb[:, cc, :], denb[:, cc, :], AF.Exp, scale=c_mone[:], ev=f"rec_{i}_{cc}")

                for dd in range(CC):
                    b.w(en, f"k_{i}_{dd}")
                    if dd == 0:
                        b.w(en, f"xnm_{i}")
                    b.op(en, "activation",
                         ek[:, dd, :], ps_k[:], AF.Exp, ev=f"ek_{i}_{dd}")
                    b.w(en, f"r_{i}_{dd}")
                    if i >= 1 and dd == 0:
                        b.w(en, f"denp_{i - 1}_{CC - 1}")
                    b.op(en, "activation",
                         erb[:, dd, :], ps_r[:], AF.Exp, scale=c_mone[:], ev=f"sr_{i}_{dd}")
                    if dd == 1:
                        b.w(en, f"tg_{i}")
                        b.op(en, "copy", gT_s[:], ps_g[0:8, 256:512], ev=f"gT_{i}")
                        b.w(en, f"ee_{i}")
                        if i >= 1:
                            b.w(en, f"gh_{i - 1}")
                        b.op(en, "copy", gateb_s[:], ps_g[:, 0:256], ev=f"gB_{i}")
                    if dd >= 1:
                        rec_op(dd - 1)
                    if dd == 6 and i + 1 < NT:
                        xsq_op(i + 1)
                    if dd >= 3:
                        silu_op(dd - 3)
                silu_op(5)
                if i + 1 < NT:
                    mu2_op(i + 1)
                silu_op(6)
                if i + 1 < NT:
                    ln_op(i + 1)
                    rstd_op(i + 1)
                silu_op(7)

        def gen_tensor(b):
            en = "tensor"
            b.w(en, "c_one")
            b.w(en, "c_ones1")

            def wo_grp(j, dd):
                if dd == 0:
                    if j == 0:
                        b.w(en, f"cast_{NQ - 1}")
                    elif j >= 1:
                        b.w(en, f"t2_{j - 1}_{CC - 1}")
                    b.w(en, f"y_{j}")
                else:
                    b.w(en, f"t2_{j}_{dd - 1}")
                for cc in range(CC):
                    b.op(en, "matmul",
                         ps_s[:, (dd % 2) * 256:(dd % 2) * 256 + 256],
                         wo[:, cc, dd * 128:(dd + 1) * 128], zb[:, cc, :],
                         start=(cc == 0), stop=(cc == CC - 1),
                         ev=f"wo_{j}_{dd}" if cc == CC - 1 else None)

            def base_grp(i, dd):
                if dd <= 1:
                    if i == 0:
                        b.w(en, "cast_15")
                        b.w(en, "cast_B")
                    else:
                        b.w(en, f"silu_{i - 1}_{CC - 2 + dd}")
                else:
                    b.w(en, f"silu_{i}_{dd - 2}")
                for cc in range(CC):
                    b.op(en, "matmul",
                         ps_base(dd),
                         wf[:, cc, dd * 128:(dd + 1) * 128],
                         xn[i % 2][:, cc, 2:2 + TT], start=(cc == 0), stop=False)
                if dd == 0:
                    b.w(en, f"gh_{i}")
                b.op(en, "matmul",
                     ps_base(dd),
                     bt[:, dd * 128:(dd + 1) * 128], ghb[:], start=False, stop=True,
                     ev=f"wb_{i}_{dd}")

            def s12_grp(j):
                b.w(en, f"xbf_{j}")
                if j >= 1:
                    b.w(en, f"mu2_{j - 1}")
                    b.w(en, f"mur_{j - 1}")
                    b.w(en, f"var_{j - 1}")
                    b.w(en, f"gT_{j - 1}")
                    b.w(en, f"gB_{j - 1}")
                for cc in range(CC):
                    b.op(en, "matmul", s1_ps, onecol[:], x_bf[:, cc, :],
                         start=(cc == 0), stop=(cc == CC - 1))
                b.w(en, f"xsq_{j}")
                for cc in range(CC):
                    b.op(en, "matmul", s2_ps, onecol[:], xsq[:, cc, :],
                         start=(cc == 0), stop=(cc == CC - 1),
                         ev=f"s12_{j}" if cc == CC - 1 else None)

            def bc_grp(j):
                b.w(en, f"rstd_{j}")
                b.w(en, f"mur_{j}")
                b.w(en, f"mu2_{j}")
                b.op(en, "matmul", ps_bc[:, 0:256], ones1[:], rstd_r[:], start=True, stop=True)
                b.op(en, "matmul", ps_bc[:, 256:512], ones1[:], mur_r[:],
                     start=True, stop=True, ev=f"bc_{j}")

            for i in range(NT):
                j = i - 1
                if i == 0:
                    s12_grp(0)
                bc_grp(i)
                if i >= 1:
                    wo_grp(j, 0)
                    wo_grp(j, 1)
                    wo_grp(j, 2)
                b.w(en, f"xnm_{i}")
                if i == 0:
                    b.w(en, "cast_rt")
                if i >= 1:
                    b.w(en, f"lts_{i - 1}")
                    b.w(en, f"gh_{i - 1}")
                for tk in range(2):
                    for cc in range(CC):
                        b.op(en, "matmul",
                             ps_h[:, 256 + tk * 8:256 + tk * 8 + 8],
                             xn[i % 2][:, cc, 2 + tk * 128:2 + tk * 128 + 128],
                             rt[:, cc, :], start=(cc == 0), stop=(cc == CC - 1),
                             ev=f"lt_{i}" if (tk == 1 and cc == CC - 1) else None)
                if i == 0:
                    b.w(en, "cast_11")
                wo_sched = {0: 3, 1: 5, 2: 6, 3: 7}
                for dd in range(CC):
                    if i >= 1 and dd in wo_sched:
                        wo_grp(j, wo_sched[dd])
                    if dd == 0:
                        b.w(en, f"mixk_{i}")
                    if dd >= 1:
                        b.w(en, f"ek_{i}_{dd - 1}")
                    elif i >= 1:
                        b.w(en, f"ek_{i - 1}_{CC - 1}")
                    for cc in range(CC):
                        b.op(en, "matmul",
                             ps_k[:], wk[:, cc, dd * 128:(dd + 1) * 128],
                             xk[:, cc, :], start=(cc == 0), stop=(cc == CC - 1),
                             ev=f"k_{i}_{dd}" if cc == CC - 1 else None)
                    if dd == 0:
                        b.w(en, f"mixv_{i}")
                    if dd >= 1:
                        b.w(en, f"ekv_{i}_{dd - 1}")
                    elif i >= 1:
                        b.w(en, f"ekv_{i - 1}_{CC - 1}")
                    for cc in range(CC):
                        b.op(en, "matmul",
                             ps_v[:], wv[:, cc, dd * 128:(dd + 1) * 128],
                             xv[:, cc, :], start=(cc == 0), stop=(cc == CC - 1),
                             ev=f"v_{i}_{dd}" if cc == CC - 1 else None)
                    if dd == 0:
                        b.w(en, f"mixr_{i}")
                    if dd >= 1:
                        b.w(en, f"sr_{i}_{dd - 1}")
                    elif i >= 1:
                        b.w(en, f"sr_{i - 1}_{CC - 1}")
                    for cc in range(CC):
                        b.op(en, "matmul",
                             ps_r[:], wr[:, cc, dd * 128:(dd + 1) * 128],
                             xr[:, cc, :], start=(cc == 0), stop=(cc == CC - 1),
                             ev=f"r_{i}_{dd}" if cc == CC - 1 else None)
                    if dd == 0:
                        if i >= 1:
                            wo_grp(j, 4)
                    elif dd == 1:
                        if i == 0:
                            b.w(en, "cast_A")
                        b.w(en, f"lts_{i}")
                        for cc in range(CC):
                            b.op(en, "matmul",
                                 ps_h[:, 0:256], at[:, cc, :], xn[i % 2][:, cc, 2:2 + TT],
                                 start=(cc == 0), stop=(cc == CC - 1),
                                 ev=f"h_{i}" if cc == CC - 1 else None)
                        b.w(en, f"gsm_{i}")
                        b.w(en, f"evm_{i}")
                        b.w(en, f"evr_{i}")
                        if i == 0:
                            b.w(en, "ld_ident")
                            b.w(en, "ld_eexp")
                        if i >= 1:
                            b.w(en, f"gB_{i - 1}")
                        for tk in range(2):
                            b.op(en, "transpose",
                                 ps_g[0:8, 256 + tk * 128:256 + tk * 128 + 128],
                                 gate_s[:, tk * 8:tk * 8 + 8], ident[:],
                                 ev=f"tg_{i}" if tk == 1 else None)
                        b.w(en, f"gT_{i}")
                        b.op(en, "matmul", ps_g[:, 0:256], eexp[:], gT_s[:],
                             start=True, stop=True, ev=f"ee_{i}")
                    else:
                        base_grp(i, dd - 2)
                    if dd == 6 and i + 1 < NT:
                        s12_grp(i + 1)
                base_grp(i, CC - 2)
                base_grp(i, CC - 1)
            for dd in range(CC):
                wo_grp(NT - 1, dd)

        def gen_gpsimd(b):
            en = "gpsimd"
            b.op(en, "memset", onesb[:], 1.0, ev="c_onesb")
            for i in range(NT):
                if i >= 1:
                    if i == 1:
                        b.w(en, "cast_A")
                        b.w(en, "cast_B")
                    for dd in range(3):
                        b.w(en, f"t2_{i - 1}_{dd}")
                        if i >= 2:
                            b.w(en, f"st_{i - 2}_{dd}")
                        b.op(en, "tensor_tensor",
                             ob[:, dd, :], t2[:, dd, :], xb[(i - 1) % 2][:, dd, :],
                             ALU.add, ev=f"out_{i - 1}_{dd}")
                out_sched = {0: [3], 1: [4, 5], 2: [6], 3: [7]}
                if i == 0:
                    b.ws(en, "c_onesb")
                for cc in range(CC):
                    b.w(en, f"sr_{i}_{cc}")
                    if i >= 1 and cc == 0:
                        b.w(en, f"denp_{i - 1}_{CC - 1}")
                    b.op(en, "tensor_tensor",
                         erb[:, cc, :], erb[:, cc, :], onesb[:], ALU.add, ev=f"oner_{i}_{cc}")
                    if cc < CC - 1:
                        b.w(en, f"den_{i}_{cc}")
                        b.ws(en, f"oner_{i}_{cc}")
                        b.op(en, "tensor_tensor",
                             denb[:, cc, :], denb[:, cc, :], erb[:, cc, :],
                             ALU.mult, ev=f"denp_{i}_{cc}")
                    if i >= 1 and cc in out_sched:
                        for dd in out_sched[cc]:
                            b.w(en, f"t2_{i - 1}_{dd}")
                            if i >= 2:
                                b.w(en, f"st_{i - 2}_{dd}")
                            b.op(en, "tensor_tensor",
                                 ob[:, dd, :], t2[:, dd, :], xb[(i - 1) % 2][:, dd, :],
                                 ALU.add, ev=f"out_{i - 1}_{dd}")
                    if cc >= 1:
                        b.w(en, f"rec_{i}_{cc - 1}")
                        b.w(en, f"num_{i}_{cc - 1}")
                        if cc == 1 and i >= 1:
                            b.w(en, f"wo_{i - 1}_{CC - 1}")
                        b.op(en, "tensor_tensor",
                             zb[:, cc - 1, :], numb[:, cc - 1, :], denb[:, cc - 1, :],
                             ALU.mult, ev=f"z_{i}_{cc - 1}")
            i = NT
            for dd in range(CC):
                b.w(en, f"t2_{i - 1}_{dd}")
                b.w(en, f"st_{i - 2}_{dd}")
                b.op(en, "tensor_tensor",
                     ob[:, dd, :], t2[:, dd, :], xb[(i - 1) % 2][:, dd, :],
                     ALU.add, ev=f"out_{i - 1}_{dd}")

        def gen_all(b):
            gen_sync(b)
            gen_vector(b)
            gen_scalar(b)
            gen_tensor(b)
            gen_gpsimd(b)

        gen_all(b)
        b.start_emit()
        with ExitStack() as semstack:
            def mksem(name):
                return semstack.enter_context(nc.semaphore(name))
            b.sem = {"sync": mksem("sm_sync"), "vector": mksem("sm_vec"),
                     "scalar": mksem("sm_sca"), "tensor": mksem("sm_ten"),
                     "gpsimd": mksem("sm_gps")}
            for dj in range(Bld.NDMA):
                b.sem[f"dma{dj}"] = mksem(f"sm_dma{dj}")
            block = semstack.enter_context(nc.Block())

            @block.sync
            def _(sync):
                b.eng["sync"] = sync
                gen_sync(b)

            @block.vector
            def _(vector):
                b.eng["vector"] = vector
                gen_vector(b)

            @block.scalar
            def _(scalar):
                b.eng["scalar"] = scalar
                gen_scalar(b)

            @block.tensor
            def _(tensor):
                b.eng["tensor"] = tensor
                gen_tensor(b)

            @block.gpsimd
            def _(gpsimd):
                b.eng["gpsimd"] = gpsimd
                gen_gpsimd(b)

    return nc


def kernel(L0_attn_x, L0_attn_time_decay, L0_attn_time_first,
           L0_attn_time_mix_k, L0_attn_time_mix_v, L0_attn_time_mix_r,
           L0_attn_W_key, L0_attn_W_value, L0_attn_W_receptance,
           L0_attn_W_output, L0_ffn_W, L0_ffn_b, L0_ffn_R, L0_ffn_A,
           L0_ffn_B):
    x = np.asarray(L0_attn_x)
    assert x.shape == (16, 2048, 1024)

    def f32c(a):
        return np.ascontiguousarray(np.asarray(a), dtype=np.float32)

    x_t = np.ascontiguousarray(x.transpose(0, 2, 1))
    wkT = f32c(np.asarray(L0_attn_W_key).T)
    wvT = f32c(np.asarray(L0_attn_W_value).T)
    wrT = f32c(np.asarray(L0_attn_W_receptance).T)
    woT = f32c(np.asarray(L0_attn_W_output).T)
    wfT = f32c(np.asarray(L0_ffn_W).T)
    aT = f32c(np.asarray(L0_ffn_A).reshape(128, C).T)
    bT = f32c(np.asarray(L0_ffn_B).transpose(0, 2, 1).reshape(128, C))
    rT = f32c(np.asarray(L0_ffn_R).T)

    def pc(a):
        return np.ascontiguousarray(np.asarray(a, dtype=np.float32).reshape(CC, P).T)

    tmk = pc(np.asarray(L0_attn_time_mix_k).reshape(-1))
    tmv = pc(np.asarray(L0_attn_time_mix_v).reshape(-1))
    tmr = pc(np.asarray(L0_attn_time_mix_r).reshape(-1))
    td = pc(L0_attn_time_decay)
    tf = pc(L0_attn_time_first)
    bias = pc(L0_ffn_b)
    ident = np.eye(P, dtype=np.float32)
    eexp = np.zeros((8, 128), dtype=np.float32)
    for e in range(8):
        eexp[e, e * 16:(e + 1) * 16] = 1.0

    nc = build_nc()
    in_maps = []
    for core in range(8):
        in_maps.append({
            "xt": np.ascontiguousarray(x_t[core * 2:core * 2 + 2]),
            "wkT": wkT, "wvT": wvT, "wrT": wrT, "woT": woT, "wfT": wfT,
            "aT": aT, "bT": bT, "rT": rT,
            "tmk": tmk, "tmv": tmv, "tmr": tmr, "td": td, "tf": tf,
            "bias": bias, "ident": ident, "eexp": eexp,
        })
    res = run_bass_kernel_spmd(nc, in_maps, core_ids=list(range(8)))
    out_t = np.concatenate([r["out_t"] for r in res.results], axis=0)
    return np.ascontiguousarray(out_t.transpose(0, 2, 1))




# version 32
# speedup vs baseline: 1.0414x; 1.0414x over previous
from collections import defaultdict
from contextlib import ExitStack

import numpy as np
import concourse.bass as bass
import concourse.mybir as mybir
from concourse.bass_utils import run_bass_kernel_spmd

F32 = mybir.dt.float32
BF16 = mybir.dt.bfloat16
AF = mybir.ActivationFunctionType
ALU = mybir.AluOpType
AX = mybir.AxisListType

P = 128
CC = 8
TT = 256
NSEQ = 2
NTS = 8
NT = NSEQ * NTS
C = 1024
EPS = 1e-5
GEPS = 1e-6


class Bld:

    NDMA = 8

    def __init__(self):
        self.mode = "record"
        self.cnt = defaultdict(int)
        self.ev = {}
        self.eng = {}
        self.sem = {}
        self.lastw = defaultdict(int)
        self.dma_cnt = 0

    def start_emit(self):
        self.mode = "emit"
        self.cnt = defaultdict(int)
        self.lastw = defaultdict(int)
        self.dma_cnt = 0

    def op(self, en, meth, *a, ev=None, inc=1, dma=False, **kw):
        if dma:
            key = f"dma{self.dma_cnt % self.NDMA}"
            self.dma_cnt += 1
            inc = 16
        else:
            key = en
        if self.mode == "record":
            if ev is not None:
                self.cnt[key] += inc
                self.ev[ev] = (key, self.cnt[key])
            return None
        if dma and self.cnt[key] >= 16:
            self.wge(en, key, self.cnt[key])
        instr = getattr(self.eng[en], meth)(*a, **kw)
        if ev is not None:
            self.cnt[key] += inc
            assert self.ev[ev] == (key, self.cnt[key]), (ev, self.ev[ev], key, self.cnt[key])
            instr.then_inc(self.sem[key], inc)
        return instr

    def w(self, en, ev, force=False):
        if self.mode == "record":
            assert isinstance(ev, str)
            return
        pen, val = self.ev[ev]
        if pen == en and not force:
            return
        if self.lastw[(en, pen)] >= val:
            return
        self.eng[en].wait_ge(self.sem[pen], val)
        self.lastw[(en, pen)] = val

    def ws(self, en, ev):
        self.w(en, ev, force=True)

    def wge(self, en, pen, val):
        if self.mode == "record":
            return
        if pen == en or self.lastw[(en, pen)] >= val:
            return
        self.eng[en].wait_ge(self.sem[pen], val)
        self.lastw[(en, pen)] = val

    def final_wait(self, en, ev):
        if self.mode == "record":
            return
        pen, val = self.ev[ev]
        self.eng[en].wait_ge(self.sem[pen], val)


def build_nc():
    nc = bass.Bass("TRN2", target_bir_lowering=False, debug=False)

    d_x = nc.dram_tensor("xt", [NSEQ, C, 2048], F32, kind="ExternalInput").ap()
    d_out = nc.dram_tensor("out_t", [NSEQ, C, 2048], F32, kind="ExternalOutput").ap()
    d_wk = nc.dram_tensor("wkT", [C, C], F32, kind="ExternalInput").ap()
    d_wv = nc.dram_tensor("wvT", [C, C], F32, kind="ExternalInput").ap()
    d_wr = nc.dram_tensor("wrT", [C, C], F32, kind="ExternalInput").ap()
    d_wo = nc.dram_tensor("woT", [C, C], F32, kind="ExternalInput").ap()
    d_wf = nc.dram_tensor("wfT", [C, C], F32, kind="ExternalInput").ap()
    d_at = nc.dram_tensor("aT", [C, 128], F32, kind="ExternalInput").ap()
    d_bt = nc.dram_tensor("bT", [128, C], F32, kind="ExternalInput").ap()
    d_rt = nc.dram_tensor("rT", [C, 8], F32, kind="ExternalInput").ap()
    d_tmk = nc.dram_tensor("tmk", [P, CC], F32, kind="ExternalInput").ap()
    d_tmv = nc.dram_tensor("tmv", [P, CC], F32, kind="ExternalInput").ap()
    d_tmr = nc.dram_tensor("tmr", [P, CC], F32, kind="ExternalInput").ap()
    d_td = nc.dram_tensor("td", [P, CC], F32, kind="ExternalInput").ap()
    d_tf = nc.dram_tensor("tf", [P, CC], F32, kind="ExternalInput").ap()
    d_bias = nc.dram_tensor("bias", [P, CC], F32, kind="ExternalInput").ap()
    d_ident = nc.dram_tensor("ident", [P, P], F32, kind="ExternalInput").ap()
    d_eexp = nc.dram_tensor("eexp", [8, 128], F32, kind="ExternalInput").ap()

    with ExitStack() as stack:
        def sb(name, shape, dtype=F32):
            return stack.enter_context(nc.sbuf_tensor(name, shape, dtype))

        def psum(name, shape):
            return stack.enter_context(nc.psum_tensor(name, shape, F32))

        wk = sb("wk", [P, CC, C], BF16)
        wv = sb("wv", [P, CC, C], BF16)
        wr = sb("wr", [P, CC, C], BF16)
        wo = sb("wo", [P, CC, C], BF16)
        wf = sb("wf", [P, CC, C], BF16)
        at = sb("at", [P, CC, 128], BF16)
        bt = sb("bt", [P, C], BF16)
        rt = sb("rt", [P, CC, 8], BF16)
        ident = sb("identt", [P, P], F32)
        eexp = sb("eexpt", [8, 128], F32)
        tmk1 = sb("tmk1", [P, CC], F32)
        tmv1 = sb("tmv1", [P, CC], F32)
        tmr1 = sb("tmr1", [P, CC], F32)
        ew = sb("ew", [P, CC], F32)
        eu = sb("eu", [P, CC], F32)
        bias = sb("biast", [P, CC], F32)
        c_tmk = sb("c_tmk", [P, CC], F32)
        c_tmv = sb("c_tmv", [P, CC], F32)
        c_tmr = sb("c_tmr", [P, CC], F32)
        c_td = sb("c_td", [P, CC], F32)
        c_tf = sb("c_tf", [P, CC], F32)
        onecol = sb("onecol", [P, 1], BF16)
        onesb = sb("onesb", [P, TT], BF16)
        c_eps = sb("c_eps", [P, 1], F32)
        c_mhalf = sb("c_mhalf", [P, 1], F32)
        c_mone = sb("c_mone", [P, 1], F32)
        ones1 = sb("ones1", [1, P], F32)
        rt_f = sb("rt_f", [P, CC, 8], F32)
        negb = sb("negb", [P, CC], F32)

        pb = sb("pb", [P, CC, TT + 1], F32)
        qb = sb("qb", [P, CC, TT + 1], F32)

        xb = [sb(f"xb{j}", [P, CC, TT], F32) for j in range(2)]
        x_bf = sb("x_bf", [P, CC, TT], BF16)
        xn = [sb(f"xn{j}", [P, CC, TT + 2], BF16) for j in range(2)]
        xsq = sb("xsq", [P, CC, TT], BF16)
        xk = sb("xk", [P, CC, TT], BF16)
        xv = sb("xv", [P, CC, TT], BF16)
        xr = sb("xr", [P, CC, TT], BF16)
        ek = sb("ek", [P, CC, TT], F32)
        ekv = sb("ekv", [P, CC, TT], F32)
        numb = sb("numb", [P, CC, TT], F32)
        denb = sb("denb", [P, CC, TT], F32)
        erb = sb("erb", [P, CC, TT], BF16)
        zb = sb("zb", [P, CC, TT], BF16)
        t2 = ekv
        ob = sb("ob", [P, CC, TT], F32)
        gb = numb
        ghb = sb("ghb", [P, TT], BF16)
        gateb_s = sb("gateb_s", [P, TT], F32)
        gate_s = sb("gate_s", [P, 16], F32)
        lt_s = sb("lt_s", [P, 16], F32)
        gT_s = sb("gT_s", [8, TT], F32)
        rstdb_s = sb("rstdb_s", [P, TT], F32)
        murb_s = sb("murb_s", [P, TT], F32)
        mu2_r = sb("mu2_r", [1, TT], F32)
        var_r = sb("var_r", [1, TT], F32)
        rstd_r = sb("rstd_r", [1, TT], F32)
        mur_r = sb("mur_r", [1, TT], F32)
        m1_s = sb("m1_s", [P, 2], F32)
        m2_s = sb("m2_s", [P, 2], F32)
        graw_s = sb("graw_s", [P, 16], F32)
        msk1_s = sb("msk1_s", [P, 16], F32)
        l2_s = sb("l2_s", [P, 16], F32)
        msk_s = sb("msk_s", [P, 16], F32)
        tg_s = sb("tg_s", [P, 16], F32)
        sall_s = sb("sall_s", [P, 2], F32)
        st_s = sb("st_s", [P, 2], F32)
        den2_s = sb("den2_s", [P, 2], F32)
        rec2_s = sb("rec2_s", [P, 2], F32)

        ps_k = psum("ps_k", [P, 256])
        ps_v = psum("ps_v", [P, 256])
        ps_r = psum("ps_r", [P, 256])
        ps_base0 = psum("ps_base0", [P, 512])
        ps_base1 = psum("ps_base1", [P, 512])
        ps_s = psum("ps_s", [P, 512])
        ps_bc = psum("ps_bc", [P, 512])
        ps_h = psum("ps_h", [P, 512])
        ps_g = ps_bc
        s1_ps = ps_bc[0:1, 0:256]
        s2_ps = ps_bc[0:1, 256:512]

        def ps_base(m):
            return (ps_base0 if m % 2 == 0 else ps_base1)[:, 0:256]

        b = Bld()

        scr = [ekv[:], numb[:], denb[:],
               pb.ap().rearrange("p cc t -> p (cc t)")[:, 0:CC * TT],
               qb.ap().rearrange("p cc t -> p (cc t)")[:, 0:CC * TT]]
        bigW = [(d_wk, wk), (d_wv, wv), (d_wr, wr), (d_wf, wf), (d_wo, wo)]
        NQ = len(bigW) * 4

        ACT_QS = [1, 3, 5, 7, 9, 11]
        DVE_QS = [q for q in range(NQ) if q not in ACT_QS]

        def x_dram(i):
            bq, tq = divmod(i, NTS)
            return d_x[bq].rearrange("(cc p) t -> p cc t", p=P)[:, :, tq * TT:(tq + 1) * TT]

        def o_dram(i):
            bq, tq = divmod(i, NTS)
            return d_out[bq].rearrange("(cc p) t -> p cc t", p=P)[:, :, tq * TT:(tq + 1) * TT]

        def gen_sync(b):
            en = "sync"
            for i in range(2):
                b.op(en, "dma_start", out=xb[i % 2][:], in_=x_dram(i), ev=f"ldx_{i}", dma=True)
            for nm, dst, src in [("tmk", c_tmk, d_tmk), ("tmv", c_tmv, d_tmv),
                                 ("tmr", c_tmr, d_tmr), ("td", c_td, d_td),
                                 ("tf", c_tf, d_tf), ("bias", bias, d_bias),
                                 ("ident", ident, d_ident), ("eexp", eexp, d_eexp)]:
                b.op(en, "dma_start", out=dst[:], in_=src, ev=f"ld_{nm}", dma=True)
            b.op(en, "dma_start", out=rt_f[:],
                 in_=d_rt.rearrange("(cc p) e -> p cc e", p=P), ev="ld_rt", dma=True)
            b.op(en, "dma_start",
                 out=ob[:, :, 0:128],
                 in_=d_at.rearrange("(cc p) e -> p cc e", p=P), ev="ld_A", dma=True)
            b.op(en, "dma_start",
                 out=ob[:, :, 128:256],
                 in_=d_bt.rearrange("p (cc e) -> p cc e", cc=CC), ev="ld_B", dma=True)
            for q in range(NQ):
                m, quarter = divmod(q, 4)
                src, _ = bigW[m]
                if q >= 5:
                    b.w(en, f"cast_{q - 5}")
                b.op(en, "dma_start",
                     out=scr[q % 5],
                     in_=src.rearrange("(cc p) d -> p cc d", p=P)[:, quarter * 2:quarter * 2 + 2, :], ev=f"ldq_{q}", dma=True)
            for i in range(2, NT + 2):
                if i < NT:
                    b.w(en, f"out_{i - 2}_{CC - 1}")
                    b.op(en, "dma_start", out=xb[i % 2][:], in_=x_dram(i), ev=f"ldx_{i}", dma=True)
                j = i - 2
                od = o_dram(j)
                for dd in range(CC):
                    b.w(en, f"out_{j}_{dd}")
                    b.op(en, "dma_start", out=od[:, dd:dd + 1, :], in_=ob[:, dd:dd + 1, :],
                         ev=f"st_{j}_{dd}", dma=True)
            for dd in range(CC):
                b.final_wait(en, f"st_{NT - 1}_{dd}")

        def gen_vector(b):
            en = "vector"
            b.op(en, "memset", onecol[:], 1.0 / 1024.0, ev="c_one")
            b.op(en, "memset", ones1[:], 1.0, ev="c_ones1")
            b.op(en, "memset", c_eps[:], EPS, ev="c_eps")
            b.op(en, "memset", c_mhalf[:], -0.5, ev="c_mhalf")
            b.op(en, "memset", c_mone[:], -1.0, ev="c_mone")
            b.w(en, "ld_rt")
            b.op(en, "tensor_copy", rt[:], rt_f[:], ev="cast_rt")
            for nm, dst, src in [("tmk", tmk1, c_tmk), ("tmv", tmv1, c_tmv), ("tmr", tmr1, c_tmr)]:
                b.w(en, f"ld_{nm}")
                b.op(en, "tensor_scalar_add", dst[:], src[:], -1.0, ev=f"c_{nm}")
            b.w(en, "ld_bias")
            b.op(en, "tensor_scalar_mul", negb[:], bias[:], -1.0, ev="c_negb")

            def weight_casts():
                for q in DVE_QS:
                    m, quarter = divmod(q, 4)
                    _, dst = bigW[m]
                    b.w(en, f"ldq_{q}")
                    b.op(en, "tensor_copy",
                         dst[:, quarter * 2:quarter * 2 + 2, :], scr[q % 5], ev=f"cast_{q}")
                b.w(en, "ld_A")
                b.op(en, "tensor_copy", at[:, :, :], ob[:, :, 0:128], ev="cast_A")
                b.w(en, "ld_B")
                b.op(en, "tensor_copy",
                     bt.ap().rearrange("p (cc e) -> p cc e", cc=CC),
                     ob[:, :, 128:256], ev="cast_B")

            def xbf_op(j):
                b.w(en, f"ldx_{j}")
                if j >= 1:
                    b.w(en, f"s12_{j - 1}")
                    b.w(en, f"xsq_{j - 1}")
                b.op(en, "tensor_copy",
                     x_bf.ap().rearrange("p cc t -> p (cc t)"),
                     xb[j % 2].ap().rearrange("p cc t -> p (cc t)"), ev=f"xbf_{j}")

            for i in range(NT):
                if i == 0:
                    xbf_op(0)
                def t2_ops(dds):
                    j2 = i - 1
                    for dd2 in dds:
                        b.w(en, f"wo_{j2}_{dd2}")
                        b.w(en, f"silu_{j2}_{dd2}")
                        b.ws(en, f"num_{j2}_{dd2}")
                        b.op(en, "tensor_tensor",
                             t2[:, dd2, :], ps_s[:, (dd2 % 2) * 256:(dd2 % 2) * 256 + 256],
                             gb[:, dd2, :], ALU.mult, ev=f"t2_{j2}_{dd2}")
                def var_op(j):
                    b.w(en, f"mu2_{j}")
                    if j >= 1:
                        b.w(en, f"rstd_{j - 1}")
                        b.w(en, f"bc_{j - 1}")
                    b.op(en, "tensor_sub", var_r[:], s2_ps, mu2_r[:], ev=f"var_{j}")

                def mur_op(j):
                    b.w(en, f"rstd_{j}")
                    b.op(en, "tensor_mul", mur_r[:], s1_ps, rstd_r[:], ev=f"mur_{j}")

                if i == 0:
                    var_op(0)
                    mur_op(0)
                b.w(en, f"evm_{i}")
                if i >= 1:
                    b.ws(en, f"den_{i - 1}_{CC - 1}")
                b.op(en, "tensor_tensor",
                     ek.ap(), xb[i % 2].ap(),
                     murb_s.ap().unsqueeze(1).broadcast_to([P, CC, TT]),
                     ALU.subtract, ev=f"d1_{i}")
                if i >= 1:
                    t2_ops([0])
                b.ws(en, f"d1_{i}")
                b.w(en, f"evr_{i}")
                if i >= 2:
                    b.w(en, f"wb_{i - 2}_{CC - 1}")
                b.op(en, "tensor_tensor",
                     xn[i % 2][:, :, 2:2 + TT], ek.ap(),
                     rstdb_s.ap().unsqueeze(1).broadcast_to([P, CC, TT]), ALU.mult, ev=f"xnm_{i}")
                if i % NTS == 0:
                    b.op(en, "memset", xn[i % 2][:, :, 1:2], 0.0, ev=f"xnb_{i}")
                else:
                    b.ws(en, f"xnm_{i - 1}")
                    b.op(en, "tensor_copy",
                         xn[i % 2][:, :, 1:2], xn[(i - 1) % 2][:, :, TT + 1:TT + 2], ev=f"xnb_{i}")
                if i >= 1:
                    t2_ops([1])
                b.w(en, f"s12_{i}")
                b.ws(en, f"xnm_{i}")
                b.ws(en, f"xnb_{i}")
                b.op(en, "tensor_tensor",
                     xsq.ap(), xn[i % 2][:, :, 2:2 + TT], xn[i % 2][:, :, 1:1 + TT],
                     ALU.subtract, ev=f"dmix_{i}")
                if i >= 1:
                    t2_ops([2])
                b.ws(en, f"dmix_{i}")
                if i == 0:
                    b.ws(en, "c_tmk")
                    b.ws(en, "c_tmv")
                    b.ws(en, "c_tmr")
                if i >= 1:
                    b.w(en, f"r_{i - 1}_{CC - 1}")
                for cc in range(CC):
                    b.op(en, "scalar_tensor_tensor",
                         xk[:, cc, :], xsq[:, cc, :], tmk1[:, cc:cc + 1],
                         xn[i % 2][:, cc, 2:2 + TT], ALU.mult, ALU.add,
                         ev=f"mixk_{i}" if cc == CC - 1 else None)
                gate_steps = []

                def gstep(k):
                    if k < len(gate_steps):
                        gate_steps[k]()

                def _g0():
                    b.w(en, f"lts_{i}")
                    if i >= 1:
                        b.w(en, f"tg_{i - 1}")
                    b.op(en, "reduce_max", m1_s[:], lt_s.ap().rearrange("p (c e) -> p c e", c=2),
                         axis=AX.X, ev=f"gm1_{i}")

                def _g1():
                    b.ws(en, f"gm1_{i}")
                    b.op(en, "tensor_tensor", msk1_s.ap().rearrange("p (c e) -> p c e", c=2),
                         lt_s.ap().rearrange("p (c e) -> p c e", c=2),
                         m1_s.ap().unsqueeze(2).broadcast_to([P, 2, 8]), ALU.is_equal, ev=f"gmk1_{i}")

                def _g2():
                    b.ws(en, f"gmk1_{i}")
                    b.op(en, "scalar_tensor_tensor", l2_s[:], msk1_s[:], -1e30, lt_s[:],
                         ALU.mult, ALU.add, ev=f"gl2_{i}")

                def _g3():
                    b.ws(en, f"gl2_{i}")
                    b.op(en, "reduce_max", m2_s[:], l2_s.ap().rearrange("p (c e) -> p c e", c=2),
                         axis=AX.X, ev=f"gm2_{i}")

                def _g4():
                    b.ws(en, f"gm2_{i}")
                    b.op(en, "tensor_tensor", msk_s.ap().rearrange("p (c e) -> p c e", c=2),
                         lt_s.ap().rearrange("p (c e) -> p c e", c=2),
                         m2_s.ap().unsqueeze(2).broadcast_to([P, 2, 8]), ALU.is_ge, ev=f"gmsk_{i}")

                def _g5():
                    b.w(en, f"graw_{i}")
                    b.ws(en, f"gmsk_{i}")
                    b.op(en, "tensor_mul", tg_s[:], graw_s[:], msk_s[:], ev=f"gtg_{i}")

                def _g6():
                    b.op(en, "reduce_sum", sall_s[:], graw_s.ap().rearrange("p (c e) -> p c e", c=2),
                         axis=AX.X, ev=f"gsa_{i}")

                def _g7():
                    b.ws(en, f"gtg_{i}")
                    b.op(en, "reduce_sum", st_s[:], tg_s.ap().rearrange("p (c e) -> p c e", c=2),
                         axis=AX.X, ev=f"gss_{i}")

                def _g8():
                    b.ws(en, f"gsa_{i}")
                    b.ws(en, f"gss_{i}")
                    b.op(en, "scalar_tensor_tensor", den2_s[:], sall_s[:], GEPS, st_s[:],
                         ALU.mult, ALU.add, ev=f"gd2_{i}")

                def _g9():
                    b.ws(en, f"gd2_{i}")
                    b.op(en, "reciprocal", rec2_s[:], den2_s[:], ev=f"grc_{i}")

                def _g10():
                    b.ws(en, f"grc_{i}")
                    b.op(en, "tensor_tensor",
                         gate_s.ap().rearrange("p (c e) -> p c e", c=2),
                         tg_s.ap().rearrange("p (c e) -> p c e", c=2),
                         rec2_s.ap().unsqueeze(2).broadcast_to([P, 2, 8]), ALU.mult, ev=f"gsm_{i}")

                gate_steps = [_g0, _g1, _g2, _g3, _g4, _g5, _g6, _g7, _g8, _g9, _g10]
                gk = 0
                for cc in range(CC):
                    b.op(en, "scalar_tensor_tensor",
                         xv[:, cc, :], xsq[:, cc, :], tmv1[:, cc:cc + 1],
                         xn[i % 2][:, cc, 2:2 + TT], ALU.mult, ALU.add,
                         ev=f"mixv_{i}" if cc == CC - 1 else None)
                    gstep(gk); gk += 1
                for cc in range(CC):
                    b.op(en, "scalar_tensor_tensor",
                         xr[:, cc, :], xsq[:, cc, :], tmr1[:, cc:cc + 1],
                         xn[i % 2][:, cc, 2:2 + TT], ALU.mult, ALU.add,
                         ev=f"mixr_{i}" if cc == CC - 1 else None)
                    gstep(gk); gk += 1
                while gk < len(gate_steps):
                    gstep(gk); gk += 1
                if i == 0:
                    weight_casts()
                if i == 0:
                    b.w(en, "c_ew")
                if i >= 1:
                    b.ws(en, f"tts_{i - 1}_{CC - 1}")
                if i % NTS == 0:
                    b.op(en, "memset", pb[:, :, 0:1], 0.0)
                    b.op(en, "memset", qb[:, :, 0:1], 0.0, ev=f"carry_{i}")
                else:
                    b.op(en, "tensor_copy", pb[:, :, 0:1], pb[:, :, TT:TT + 1])
                    b.op(en, "tensor_copy", qb[:, :, 0:1], qb[:, :, TT:TT + 1], ev=f"carry_{i}")
                b.ws(en, f"carry_{i}")
                t2_pre = {0: [3], 1: [5], 3: [7]}
                t2_post = {0: [4], 1: [6]}
                g_sched = {3: 0, 4: 1, 5: 2, 6: 3, 7: 4}

                def g_op(m):
                    b.w(en, f"sbC_{i}_{m}")
                    b.op(en, "scalar_tensor_tensor",
                         gb[:, m, :], ps_base(m),
                         bias[:, m:m + 1], gb[:, m, :],
                         ALU.add, ALU.mult, ev=f"silu_{i}_{m}")

                for dd in range(CC):
                    if i >= 1 and dd in t2_pre:
                        t2_ops(t2_pre[dd])
                    if i >= 1 and dd <= 2:
                        b.w(en, f"out_{i - 1}_{dd}")
                    b.w(en, f"ek_{i}_{dd}")
                    b.w(en, f"v_{i}_{dd}")
                    if i >= 1:
                        b.ws(en, f"t2_{i - 1}_{dd}")
                        if dd > 2:
                            b.w(en, f"out_{i - 1}_{dd}")
                    b.op(en, "tensor_tensor",
                         ekv[:, dd, :], ek[:, dd, :], ps_v[:], ALU.mult, ev=f"ekv_{i}_{dd}")
                    b.ws(en, f"ekv_{i}_{dd}")
                    b.op(en, "tensor_tensor_scan",
                         pb[:, dd, 1:1 + TT], ew[:, dd:dd + 1].broadcast_to([P, TT]),
                         ekv[:, dd, :], pb[:, dd, 0:1], ALU.mult, ALU.add, ev=f"ttp_{i}_{dd}")
                    b.op(en, "tensor_tensor_scan",
                         qb[:, dd, 1:1 + TT], ew[:, dd:dd + 1].broadcast_to([P, TT]),
                         ek[:, dd, :], qb[:, dd, 0:1], ALU.mult, ALU.add, ev=f"tts_{i}_{dd}")
                    if dd == 1:
                        b.w(en, f"h_{i}")
                        b.w(en, f"gB_{i}")
                        if i >= 1:
                            b.w(en, f"wb_{i - 1}_{CC - 1}")
                        b.op(en, "tensor_tensor", ghb[:], ps_h[:, 0:256], gateb_s[:],
                             ALU.mult, ev=f"gh_{i}")
                    if dd == 0:
                        if i == 0:
                            b.w(en, "c_eu")
                        if i >= 1:
                            b.w(en, f"z_{i - 1}_{CC - 2}")
                    b.ws(en, f"ttp_{i}_{dd}")
                    b.op(en, "scalar_tensor_tensor",
                         numb[:, dd, :], ekv[:, dd, :], eu[:, dd:dd + 1],
                         pb[:, dd, 0:TT], ALU.mult, ALU.add, ev=f"num_{i}_{dd}")
                    b.ws(en, f"tts_{i}_{dd}")
                    b.op(en, "scalar_tensor_tensor",
                         denb[:, dd, :], ek[:, dd, :], eu[:, dd:dd + 1],
                         qb[:, dd, 0:TT], ALU.mult, ALU.add, ev=f"den_{i}_{dd}")
                    if i >= 1 and dd in t2_post:
                        t2_ops(t2_post[dd])
                    if dd in g_sched:
                        g_op(g_sched[dd])
                    if dd == 5 and i + 1 < NT:
                        xbf_op(i + 1)
                    if dd == CC - 1:
                        b.w(en, f"oner_{i}_{CC - 1}")
                        b.ws(en, f"den_{i}_{CC - 1}")
                        b.op(en, "tensor_tensor",
                             ek[:, CC - 1, :], denb[:, CC - 1, :], erb[:, CC - 1, :],
                             ALU.mult, ev=f"denp_{i}_{CC - 1}")
                        b.ws(en, f"denp_{i}_{CC - 1}")
                        b.op(en, "reciprocal",
                             denb[:, CC - 1, :], ek[:, CC - 1, :], ev=f"recd_{i}")
                        b.ws(en, f"recd_{i}")
                        if i >= 1:
                            b.w(en, f"wo_{i - 1}_{CC - 1}")
                        b.op(en, "tensor_tensor",
                             zb[:, CC - 1, :], numb[:, CC - 1, :], denb[:, CC - 1, :],
                             ALU.mult, ev=f"y_{i}")
                g_op(5)
                if i + 1 < NT:
                    var_op(i + 1)
                g_op(6)
                if i + 1 < NT:
                    mur_op(i + 1)
                g_op(7)
            j = NT - 1
            for dd in range(CC):
                b.w(en, f"wo_{j}_{dd}")
                b.w(en, f"silu_{j}_{dd}")
                b.ws(en, f"num_{j}_{dd}")
                b.op(en, "tensor_tensor",
                     t2[:, dd, :], ps_s[:, (dd % 2) * 256:(dd % 2) * 256 + 256],
                     gb[:, dd, :], ALU.mult, ev=f"t2_{j}_{dd}")

        def gen_scalar(b):
            en = "scalar"
            b.w(en, "ld_td")
            b.w(en, "c_mone")
            b.w(en, "c_mhalf")
            b.w(en, "c_eps")
            b.op(en, "activation", ew[:], c_td[:], AF.Exp, ev="ewt")
            b.ws(en, "ewt")
            b.op(en, "activation", ew[:], ew[:], AF.Exp, scale=c_mone[:], ev="c_ew")
            b.w(en, "ld_tf")
            b.op(en, "activation", eu[:], c_tf[:], AF.Exp, ev="c_eu")

            for i in range(NT):
                def xsq_op(j):
                    b.w(en, f"xbf_{j}")
                    if j >= 1:
                        b.w(en, f"mixr_{j - 1}")
                    b.op(en, "activation",
                         xsq.ap().rearrange("p cc t -> p (cc t)"),
                         x_bf.ap().rearrange("p cc t -> p (cc t)"), AF.Square, ev=f"xsq_{j}")

                def mu2_op(j):
                    b.w(en, f"s12_{j}")
                    if j >= 1:
                        b.w(en, f"var_{j - 1}")
                    b.op(en, "activation", mu2_r[:], s1_ps, AF.Square, ev=f"mu2_{j}")

                def ln_op(j):
                    b.w(en, f"var_{j}")
                    if j >= 1:
                        b.w(en, f"bc_{j - 1}")
                    b.op(en, "activation", rstd_r[:], var_r[:], AF.Ln,
                         bias=c_eps[0:1, :], ev=f"ln_{j}")

                def rstd_op(j):
                    b.ws(en, f"ln_{j}")
                    b.op(en, "activation", rstd_r[:], rstd_r[:], AF.Exp,
                         scale=c_mhalf[0:1, :], ev=f"rstd_{j}")

                def ev_ops(j):
                    b.w(en, f"bc_{j}")
                    if j >= 1:
                        b.w(en, f"xnm_{j - 1}")
                        b.w(en, f"d1_{j - 1}")
                    b.op(en, "copy", rstdb_s[:], ps_bc[:, 0:256], ev=f"evr_{j}")
                    b.op(en, "copy", murb_s[:], ps_bc[:, 256:512], ev=f"evm_{j}")

                if i == 0:
                    xsq_op(0)
                    mu2_op(0)
                    ln_op(0)
                    rstd_op(0)
                ev_ops(i)
                b.w(en, f"lt_{i}")
                if i >= 1:
                    b.w(en, f"gsm_{i - 1}")
                b.op(en, "copy", lt_s[:], ps_h[:, 256:272], ev=f"lts_{i}")
                b.ws(en, f"lts_{i}")
                b.op(en, "activation", graw_s[:], lt_s[:], AF.Exp, ev=f"graw_{i}")
                if i == 0:
                    for q in ACT_QS:
                        m0, quarter = divmod(q, 4)
                        _, dst = bigW[m0]
                        b.w(en, f"ldq_{q}")
                        b.op(en, "copy",
                             dst[:, quarter * 2:quarter * 2 + 2, :], scr[q % 5],
                             ev=f"cast_{q}")
                def silu_op(m):
                    half = ps_base(m)
                    b.w(en, f"wb_{i}_{m}")
                    if i == 0 and m == 0:
                        b.w(en, "c_negb")
                    if m == CC - 1:
                        b.w(en, f"y_{i}")
                    else:
                        b.w(en, f"z_{i}_{m}")
                    b.op(en, "activation", gb[:, m, :], half, AF.Exp,
                         scale=c_mone[:], bias=negb[:, m:m + 1], ev=f"sbA_{i}_{m}")
                    b.ws(en, f"sbA_{i}_{m}")
                    b.op(en, "activation", gb[:, m, :], gb[:, m, :], AF.Ln,
                         bias=1.0, ev=f"sbB_{i}_{m}")
                    b.ws(en, f"sbB_{i}_{m}")
                    b.op(en, "activation", gb[:, m, :], gb[:, m, :], AF.Exp,
                         scale=c_mone[:], ev=f"sbC_{i}_{m}")

                def rec_op(cc):
                    b.w(en, f"denp_{i}_{cc}")
                    b.op(en, "activation", denb[:, cc, :], denb[:, cc, :], AF.Ln, ev=f"lnr_{i}_{cc}")
                    b.ws(en, f"lnr_{i}_{cc}")
                    b.op(en, "activation",
                         denb[:, cc, :], denb[:, cc, :], AF.Exp, scale=c_mone[:], ev=f"rec_{i}_{cc}")

                for dd in range(CC):
                    b.w(en, f"k_{i}_{dd}")
                    if dd == 0:
                        b.w(en, f"xnm_{i}")
                    b.op(en, "activation",
                         ek[:, dd, :], ps_k[:], AF.Exp, ev=f"ek_{i}_{dd}")
                    b.w(en, f"r_{i}_{dd}")
                    if i >= 1 and dd == 0:
                        b.w(en, f"denp_{i - 1}_{CC - 1}")
                    b.op(en, "activation",
                         erb[:, dd, :], ps_r[:], AF.Exp, scale=c_mone[:], ev=f"sr_{i}_{dd}")
                    if dd == 1:
                        b.w(en, f"tg_{i}")
                        b.op(en, "copy", gT_s[:], ps_g[0:8, 256:512], ev=f"gT_{i}")
                        b.w(en, f"ee_{i}")
                        if i >= 1:
                            b.w(en, f"gh_{i - 1}")
                        b.op(en, "copy", gateb_s[:], ps_g[:, 0:256], ev=f"gB_{i}")
                    if dd >= 1:
                        rec_op(dd - 1)
                    if dd == 6 and i + 1 < NT:
                        xsq_op(i + 1)
                    if dd >= 3:
                        silu_op(dd - 3)
                silu_op(5)
                if i + 1 < NT:
                    mu2_op(i + 1)
                silu_op(6)
                if i + 1 < NT:
                    ln_op(i + 1)
                    rstd_op(i + 1)
                silu_op(7)

        def gen_tensor(b):
            en = "tensor"
            b.w(en, "c_one")
            b.w(en, "c_ones1")

            def wo_grp(j, dd):
                if dd == 0:
                    if j == 0:
                        b.w(en, f"cast_{NQ - 1}")
                    elif j >= 1:
                        b.w(en, f"t2_{j - 1}_{CC - 1}")
                    b.w(en, f"y_{j}")
                else:
                    b.w(en, f"t2_{j}_{dd - 1}")
                for cc in range(CC):
                    b.op(en, "matmul",
                         ps_s[:, (dd % 2) * 256:(dd % 2) * 256 + 256],
                         wo[:, cc, dd * 128:(dd + 1) * 128], zb[:, cc, :],
                         start=(cc == 0), stop=(cc == CC - 1),
                         ev=f"wo_{j}_{dd}" if cc == CC - 1 else None)

            def base_grp(i, dd):
                if dd <= 1:
                    if i == 0:
                        b.w(en, "cast_15")
                        b.w(en, "cast_B")
                    else:
                        b.w(en, f"silu_{i - 1}_{CC - 2 + dd}")
                else:
                    b.w(en, f"silu_{i}_{dd - 2}")
                for cc in range(CC):
                    b.op(en, "matmul",
                         ps_base(dd),
                         wf[:, cc, dd * 128:(dd + 1) * 128],
                         xn[i % 2][:, cc, 2:2 + TT], start=(cc == 0), stop=False)
                if dd == 0:
                    b.w(en, f"gh_{i}")
                b.op(en, "matmul",
                     ps_base(dd),
                     bt[:, dd * 128:(dd + 1) * 128], ghb[:], start=False, stop=True,
                     ev=f"wb_{i}_{dd}")

            def s12_grp(j):
                b.w(en, f"xbf_{j}")
                if j >= 1:
                    b.w(en, f"mu2_{j - 1}")
                    b.w(en, f"mur_{j - 1}")
                    b.w(en, f"var_{j - 1}")
                    b.w(en, f"gT_{j - 1}")
                    b.w(en, f"gB_{j - 1}")
                for cc in range(CC):
                    b.op(en, "matmul", s1_ps, onecol[:], x_bf[:, cc, :],
                         start=(cc == 0), stop=(cc == CC - 1))
                b.w(en, f"xsq_{j}")
                for cc in range(CC):
                    b.op(en, "matmul", s2_ps, onecol[:], xsq[:, cc, :],
                         start=(cc == 0), stop=(cc == CC - 1),
                         ev=f"s12_{j}" if cc == CC - 1 else None)

            def bc_grp(j):
                b.w(en, f"rstd_{j}")
                b.w(en, f"mur_{j}")
                b.w(en, f"mu2_{j}")
                b.op(en, "matmul", ps_bc[:, 0:256], ones1[:], rstd_r[:], start=True, stop=True)
                b.op(en, "matmul", ps_bc[:, 256:512], ones1[:], mur_r[:],
                     start=True, stop=True, ev=f"bc_{j}")

            for i in range(NT):
                j = i - 1
                if i == 0:
                    s12_grp(0)
                bc_grp(i)
                if i >= 1:
                    wo_grp(j, 0)
                    wo_grp(j, 1)
                    wo_grp(j, 2)
                b.w(en, f"xnm_{i}")
                if i == 0:
                    b.w(en, "cast_rt")
                if i >= 1:
                    b.w(en, f"lts_{i - 1}")
                    b.w(en, f"gh_{i - 1}")
                for tk in range(2):
                    for cc in range(CC):
                        b.op(en, "matmul",
                             ps_h[:, 256 + tk * 8:256 + tk * 8 + 8],
                             xn[i % 2][:, cc, 2 + tk * 128:2 + tk * 128 + 128],
                             rt[:, cc, :], start=(cc == 0), stop=(cc == CC - 1),
                             ev=f"lt_{i}" if (tk == 1 and cc == CC - 1) else None)
                wo_sched = {0: 3, 1: 5, 2: 6, 3: 7}
                for dd in range(CC):
                    if i >= 1 and dd in wo_sched:
                        wo_grp(j, wo_sched[dd])
                    if dd == 0:
                        b.w(en, f"mixk_{i}")
                        if i == 0:
                            b.w(en, "cast_2")
                            b.w(en, "cast_3")
                    if dd >= 1:
                        b.w(en, f"ek_{i}_{dd - 1}")
                    elif i >= 1:
                        b.w(en, f"ek_{i - 1}_{CC - 1}")
                    for cc in range(CC):
                        b.op(en, "matmul",
                             ps_k[:], wk[:, cc, dd * 128:(dd + 1) * 128],
                             xk[:, cc, :], start=(cc == 0), stop=(cc == CC - 1),
                             ev=f"k_{i}_{dd}" if cc == CC - 1 else None)
                    if dd == 0:
                        b.w(en, f"mixv_{i}")
                        if i == 0:
                            b.w(en, "cast_6")
                            b.w(en, "cast_7")
                    if dd >= 1:
                        b.w(en, f"ekv_{i}_{dd - 1}")
                    elif i >= 1:
                        b.w(en, f"ekv_{i - 1}_{CC - 1}")
                    for cc in range(CC):
                        b.op(en, "matmul",
                             ps_v[:], wv[:, cc, dd * 128:(dd + 1) * 128],
                             xv[:, cc, :], start=(cc == 0), stop=(cc == CC - 1),
                             ev=f"v_{i}_{dd}" if cc == CC - 1 else None)
                    if dd == 0:
                        b.w(en, f"mixr_{i}")
                        if i == 0:
                            b.w(en, "cast_10")
                            b.w(en, "cast_11")
                    if dd >= 1:
                        b.w(en, f"sr_{i}_{dd - 1}")
                    elif i >= 1:
                        b.w(en, f"sr_{i - 1}_{CC - 1}")
                    for cc in range(CC):
                        b.op(en, "matmul",
                             ps_r[:], wr[:, cc, dd * 128:(dd + 1) * 128],
                             xr[:, cc, :], start=(cc == 0), stop=(cc == CC - 1),
                             ev=f"r_{i}_{dd}" if cc == CC - 1 else None)
                    if dd == 0:
                        if i >= 1:
                            wo_grp(j, 4)
                    elif dd == 1:
                        if i == 0:
                            b.w(en, "cast_A")
                        b.w(en, f"lts_{i}")
                        for cc in range(CC):
                            b.op(en, "matmul",
                                 ps_h[:, 0:256], at[:, cc, :], xn[i % 2][:, cc, 2:2 + TT],
                                 start=(cc == 0), stop=(cc == CC - 1),
                                 ev=f"h_{i}" if cc == CC - 1 else None)
                        b.w(en, f"gsm_{i}")
                        b.w(en, f"evm_{i}")
                        b.w(en, f"evr_{i}")
                        if i == 0:
                            b.w(en, "ld_ident")
                            b.w(en, "ld_eexp")
                        if i >= 1:
                            b.w(en, f"gB_{i - 1}")
                        for tk in range(2):
                            b.op(en, "transpose",
                                 ps_g[0:8, 256 + tk * 128:256 + tk * 128 + 128],
                                 gate_s[:, tk * 8:tk * 8 + 8], ident[:],
                                 ev=f"tg_{i}" if tk == 1 else None)
                        b.w(en, f"gT_{i}")
                        b.op(en, "matmul", ps_g[:, 0:256], eexp[:], gT_s[:],
                             start=True, stop=True, ev=f"ee_{i}")
                    else:
                        base_grp(i, dd - 2)
                    if dd == 7 and i + 1 < NT:
                        s12_grp(i + 1)
                base_grp(i, CC - 2)
                base_grp(i, CC - 1)
            for dd in range(CC):
                wo_grp(NT - 1, dd)

        def gen_gpsimd(b):
            en = "gpsimd"
            b.op(en, "memset", onesb[:], 1.0, ev="c_onesb")
            for i in range(NT):
                if i >= 1:
                    if i == 1:
                        b.w(en, "cast_A")
                        b.w(en, "cast_B")
                    for dd in range(3):
                        b.w(en, f"t2_{i - 1}_{dd}")
                        if i >= 2:
                            b.w(en, f"st_{i - 2}_{dd}")
                        b.op(en, "tensor_tensor",
                             ob[:, dd, :], t2[:, dd, :], xb[(i - 1) % 2][:, dd, :],
                             ALU.add, ev=f"out_{i - 1}_{dd}")
                out_sched = {0: [3], 1: [4, 5], 2: [6, 7]}
                if i == 0:
                    b.ws(en, "c_onesb")
                for cc in range(CC):
                    b.w(en, f"sr_{i}_{cc}")
                    if i >= 1 and cc == 0:
                        b.w(en, f"denp_{i - 1}_{CC - 1}")
                    b.op(en, "tensor_tensor",
                         erb[:, cc, :], erb[:, cc, :], onesb[:], ALU.add, ev=f"oner_{i}_{cc}")
                    if cc < CC - 1:
                        b.w(en, f"den_{i}_{cc}")
                        b.ws(en, f"oner_{i}_{cc}")
                        b.op(en, "tensor_tensor",
                             denb[:, cc, :], denb[:, cc, :], erb[:, cc, :],
                             ALU.mult, ev=f"denp_{i}_{cc}")
                    if i >= 1 and cc in out_sched:
                        for dd in out_sched[cc]:
                            b.w(en, f"t2_{i - 1}_{dd}")
                            if i >= 2:
                                b.w(en, f"st_{i - 2}_{dd}")
                            b.op(en, "tensor_tensor",
                                 ob[:, dd, :], t2[:, dd, :], xb[(i - 1) % 2][:, dd, :],
                                 ALU.add, ev=f"out_{i - 1}_{dd}")
                    if cc >= 1:
                        b.w(en, f"rec_{i}_{cc - 1}")
                        b.w(en, f"num_{i}_{cc - 1}")
                        if cc == 1 and i >= 1:
                            b.w(en, f"wo_{i - 1}_{CC - 1}")
                        b.op(en, "tensor_tensor",
                             zb[:, cc - 1, :], numb[:, cc - 1, :], denb[:, cc - 1, :],
                             ALU.mult, ev=f"z_{i}_{cc - 1}")
            i = NT
            for dd in range(CC):
                b.w(en, f"t2_{i - 1}_{dd}")
                b.w(en, f"st_{i - 2}_{dd}")
                b.op(en, "tensor_tensor",
                     ob[:, dd, :], t2[:, dd, :], xb[(i - 1) % 2][:, dd, :],
                     ALU.add, ev=f"out_{i - 1}_{dd}")

        def gen_all(b):
            gen_sync(b)
            gen_vector(b)
            gen_scalar(b)
            gen_tensor(b)
            gen_gpsimd(b)

        gen_all(b)
        b.start_emit()
        with ExitStack() as semstack:
            def mksem(name):
                return semstack.enter_context(nc.semaphore(name))
            b.sem = {"sync": mksem("sm_sync"), "vector": mksem("sm_vec"),
                     "scalar": mksem("sm_sca"), "tensor": mksem("sm_ten"),
                     "gpsimd": mksem("sm_gps")}
            for dj in range(Bld.NDMA):
                b.sem[f"dma{dj}"] = mksem(f"sm_dma{dj}")
            block = semstack.enter_context(nc.Block())

            @block.sync
            def _(sync):
                b.eng["sync"] = sync
                gen_sync(b)

            @block.vector
            def _(vector):
                b.eng["vector"] = vector
                gen_vector(b)

            @block.scalar
            def _(scalar):
                b.eng["scalar"] = scalar
                gen_scalar(b)

            @block.tensor
            def _(tensor):
                b.eng["tensor"] = tensor
                gen_tensor(b)

            @block.gpsimd
            def _(gpsimd):
                b.eng["gpsimd"] = gpsimd
                gen_gpsimd(b)

    return nc


def kernel(L0_attn_x, L0_attn_time_decay, L0_attn_time_first,
           L0_attn_time_mix_k, L0_attn_time_mix_v, L0_attn_time_mix_r,
           L0_attn_W_key, L0_attn_W_value, L0_attn_W_receptance,
           L0_attn_W_output, L0_ffn_W, L0_ffn_b, L0_ffn_R, L0_ffn_A,
           L0_ffn_B):
    x = np.asarray(L0_attn_x)
    assert x.shape == (16, 2048, 1024)

    def f32c(a):
        return np.ascontiguousarray(np.asarray(a), dtype=np.float32)

    x_t = np.ascontiguousarray(x.transpose(0, 2, 1))
    wkT = f32c(np.asarray(L0_attn_W_key).T)
    wvT = f32c(np.asarray(L0_attn_W_value).T)
    wrT = f32c(np.asarray(L0_attn_W_receptance).T)
    woT = f32c(np.asarray(L0_attn_W_output).T)
    wfT = f32c(np.asarray(L0_ffn_W).T)
    aT = f32c(np.asarray(L0_ffn_A).reshape(128, C).T)
    bT = f32c(np.asarray(L0_ffn_B).transpose(0, 2, 1).reshape(128, C))
    rT = f32c(np.asarray(L0_ffn_R).T)

    def pc(a):
        return np.ascontiguousarray(np.asarray(a, dtype=np.float32).reshape(CC, P).T)

    tmk = pc(np.asarray(L0_attn_time_mix_k).reshape(-1))
    tmv = pc(np.asarray(L0_attn_time_mix_v).reshape(-1))
    tmr = pc(np.asarray(L0_attn_time_mix_r).reshape(-1))
    td = pc(L0_attn_time_decay)
    tf = pc(L0_attn_time_first)
    bias = pc(L0_ffn_b)
    ident = np.eye(P, dtype=np.float32)
    eexp = np.zeros((8, 128), dtype=np.float32)
    for e in range(8):
        eexp[e, e * 16:(e + 1) * 16] = 1.0

    nc = build_nc()
    in_maps = []
    for core in range(8):
        in_maps.append({
            "xt": np.ascontiguousarray(x_t[core * 2:core * 2 + 2]),
            "wkT": wkT, "wvT": wvT, "wrT": wrT, "woT": woT, "wfT": wfT,
            "aT": aT, "bT": bT, "rT": rT,
            "tmk": tmk, "tmv": tmv, "tmr": tmr, "td": td, "tf": tf,
            "bias": bias, "ident": ident, "eexp": eexp,
        })
    res = run_bass_kernel_spmd(nc, in_maps, core_ids=list(range(8)))
    out_t = np.concatenate([r["out_t"] for r in res.results], axis=0)
    return np.ascontiguousarray(out_t.transpose(0, 2, 1))




# version 33 (final)
# speedup vs baseline: 1.0426x; 1.0011x over previous
from collections import defaultdict
from contextlib import ExitStack

import numpy as np
import concourse.bass as bass
import concourse.mybir as mybir
from concourse.bass_utils import run_bass_kernel_spmd

F32 = mybir.dt.float32
BF16 = mybir.dt.bfloat16
AF = mybir.ActivationFunctionType
ALU = mybir.AluOpType
AX = mybir.AxisListType

P = 128
CC = 8
TT = 256
NSEQ = 2
NTS = 8
NT = NSEQ * NTS
C = 1024
EPS = 1e-5
GEPS = 1e-6


class Bld:

    NDMA = 8

    def __init__(self):
        self.mode = "record"
        self.cnt = defaultdict(int)
        self.ev = {}
        self.eng = {}
        self.sem = {}
        self.lastw = defaultdict(int)
        self.dma_cnt = 0

    def start_emit(self):
        self.mode = "emit"
        self.cnt = defaultdict(int)
        self.lastw = defaultdict(int)
        self.dma_cnt = 0

    def op(self, en, meth, *a, ev=None, inc=1, dma=False, **kw):
        if dma:
            key = f"dma{self.dma_cnt % self.NDMA}"
            self.dma_cnt += 1
            inc = 16
        else:
            key = en
        if self.mode == "record":
            if ev is not None:
                self.cnt[key] += inc
                self.ev[ev] = (key, self.cnt[key])
            return None
        if dma and self.cnt[key] >= 16:
            self.wge(en, key, self.cnt[key])
        instr = getattr(self.eng[en], meth)(*a, **kw)
        if ev is not None:
            self.cnt[key] += inc
            assert self.ev[ev] == (key, self.cnt[key]), (ev, self.ev[ev], key, self.cnt[key])
            instr.then_inc(self.sem[key], inc)
        return instr

    def w(self, en, ev, force=False):
        if self.mode == "record":
            assert isinstance(ev, str)
            return
        pen, val = self.ev[ev]
        if pen == en and not force:
            return
        if self.lastw[(en, pen)] >= val:
            return
        self.eng[en].wait_ge(self.sem[pen], val)
        self.lastw[(en, pen)] = val

    def ws(self, en, ev):
        self.w(en, ev, force=True)

    def wge(self, en, pen, val):
        if self.mode == "record":
            return
        if pen == en or self.lastw[(en, pen)] >= val:
            return
        self.eng[en].wait_ge(self.sem[pen], val)
        self.lastw[(en, pen)] = val

    def final_wait(self, en, ev):
        if self.mode == "record":
            return
        pen, val = self.ev[ev]
        self.eng[en].wait_ge(self.sem[pen], val)


def build_nc():
    nc = bass.Bass("TRN2", target_bir_lowering=False, debug=False)

    d_x = nc.dram_tensor("xt", [NSEQ, C, 2048], F32, kind="ExternalInput").ap()
    d_out = nc.dram_tensor("out_t", [NSEQ, C, 2048], F32, kind="ExternalOutput").ap()
    d_wk = nc.dram_tensor("wkT", [C, C], F32, kind="ExternalInput").ap()
    d_wv = nc.dram_tensor("wvT", [C, C], F32, kind="ExternalInput").ap()
    d_wr = nc.dram_tensor("wrT", [C, C], F32, kind="ExternalInput").ap()
    d_wo = nc.dram_tensor("woT", [C, C], F32, kind="ExternalInput").ap()
    d_wf = nc.dram_tensor("wfT", [C, C], F32, kind="ExternalInput").ap()
    d_at = nc.dram_tensor("aT", [C, 128], F32, kind="ExternalInput").ap()
    d_bt = nc.dram_tensor("bT", [128, C], F32, kind="ExternalInput").ap()
    d_rt = nc.dram_tensor("rT", [C, 8], F32, kind="ExternalInput").ap()
    d_tmk = nc.dram_tensor("tmk", [P, CC], F32, kind="ExternalInput").ap()
    d_tmv = nc.dram_tensor("tmv", [P, CC], F32, kind="ExternalInput").ap()
    d_tmr = nc.dram_tensor("tmr", [P, CC], F32, kind="ExternalInput").ap()
    d_td = nc.dram_tensor("td", [P, CC], F32, kind="ExternalInput").ap()
    d_tf = nc.dram_tensor("tf", [P, CC], F32, kind="ExternalInput").ap()
    d_bias = nc.dram_tensor("bias", [P, CC], F32, kind="ExternalInput").ap()
    d_ident = nc.dram_tensor("ident", [P, P], F32, kind="ExternalInput").ap()
    d_eexp = nc.dram_tensor("eexp", [8, 128], F32, kind="ExternalInput").ap()

    with ExitStack() as stack:
        def sb(name, shape, dtype=F32):
            return stack.enter_context(nc.sbuf_tensor(name, shape, dtype))

        def psum(name, shape):
            return stack.enter_context(nc.psum_tensor(name, shape, F32))

        wk = sb("wk", [P, CC, C], BF16)
        wv = sb("wv", [P, CC, C], BF16)
        wr = sb("wr", [P, CC, C], BF16)
        wo = sb("wo", [P, CC, C], BF16)
        wf = sb("wf", [P, CC, C], BF16)
        at = sb("at", [P, CC, 128], BF16)
        bt = sb("bt", [P, C], BF16)
        rt = sb("rt", [P, CC, 8], BF16)
        ident = sb("identt", [P, P], F32)
        eexp = sb("eexpt", [8, 128], F32)
        tmk1 = sb("tmk1", [P, CC], F32)
        tmv1 = sb("tmv1", [P, CC], F32)
        tmr1 = sb("tmr1", [P, CC], F32)
        ew = sb("ew", [P, CC], F32)
        eu = sb("eu", [P, CC], F32)
        bias = sb("biast", [P, CC], F32)
        c_tmk = sb("c_tmk", [P, CC], F32)
        c_tmv = sb("c_tmv", [P, CC], F32)
        c_tmr = sb("c_tmr", [P, CC], F32)
        c_td = sb("c_td", [P, CC], F32)
        c_tf = sb("c_tf", [P, CC], F32)
        onecol = sb("onecol", [P, 1], BF16)
        onesb = sb("onesb", [P, TT], BF16)
        c_eps = sb("c_eps", [P, 1], F32)
        c_mhalf = sb("c_mhalf", [P, 1], F32)
        c_mone = sb("c_mone", [P, 1], F32)
        ones1 = sb("ones1", [1, P], F32)
        rt_f = sb("rt_f", [P, CC, 8], F32)
        negb = sb("negb", [P, CC], F32)

        pb = sb("pb", [P, CC, TT + 1], F32)
        qb = sb("qb", [P, CC, TT + 1], F32)

        xb = [sb(f"xb{j}", [P, CC, TT], F32) for j in range(2)]
        x_bf = sb("x_bf", [P, CC, TT], BF16)
        xn = [sb(f"xn{j}", [P, CC, TT + 2], BF16) for j in range(2)]
        xsq = sb("xsq", [P, CC, TT], BF16)
        xk = sb("xk", [P, CC, TT], BF16)
        xv = sb("xv", [P, CC, TT], BF16)
        xr = sb("xr", [P, CC, TT], BF16)
        ek = sb("ek", [P, CC, TT], F32)
        ekv = sb("ekv", [P, CC, TT], F32)
        numb = sb("numb", [P, CC, TT], F32)
        denb = sb("denb", [P, CC, TT], F32)
        erb = sb("erb", [P, CC, TT], BF16)
        zb = sb("zb", [P, CC, TT], BF16)
        t2 = ekv
        ob = sb("ob", [P, CC, TT], F32)
        gb = numb
        ghb = sb("ghb", [P, TT], BF16)
        gateb_s = sb("gateb_s", [P, TT], F32)
        gate_s = sb("gate_s", [P, 16], F32)
        lt_s = sb("lt_s", [P, 16], F32)
        gT_s = sb("gT_s", [8, TT], F32)
        rstdb_s = sb("rstdb_s", [P, TT], F32)
        murb_s = sb("murb_s", [P, TT], F32)
        mu2_r = sb("mu2_r", [1, TT], F32)
        var_r = sb("var_r", [1, TT], F32)
        rstd_r = sb("rstd_r", [1, TT], F32)
        mur_r = sb("mur_r", [1, TT], F32)
        m1_s = sb("m1_s", [P, 2], F32)
        m2_s = sb("m2_s", [P, 2], F32)
        graw_s = sb("graw_s", [P, 16], F32)
        msk1_s = sb("msk1_s", [P, 16], F32)
        l2_s = sb("l2_s", [P, 16], F32)
        msk_s = sb("msk_s", [P, 16], F32)
        tg_s = sb("tg_s", [P, 16], F32)
        sall_s = sb("sall_s", [P, 2], F32)
        st_s = sb("st_s", [P, 2], F32)
        den2_s = sb("den2_s", [P, 2], F32)
        rec2_s = sb("rec2_s", [P, 2], F32)

        ps_k = psum("ps_k", [P, 256])
        ps_v = psum("ps_v", [P, 256])
        ps_r = psum("ps_r", [P, 256])
        ps_base0 = psum("ps_base0", [P, 512])
        ps_base1 = psum("ps_base1", [P, 512])
        ps_s = psum("ps_s", [P, 512])
        ps_bc = psum("ps_bc", [P, 512])
        ps_h = psum("ps_h", [P, 512])
        ps_g = ps_bc
        s1_ps = ps_bc[0:1, 0:256]
        s2_ps = ps_bc[0:1, 256:512]

        def ps_base(m):
            return (ps_base0 if m % 2 == 0 else ps_base1)[:, 0:256]

        b = Bld()

        scr = [ekv[:], numb[:], denb[:],
               pb.ap().rearrange("p cc t -> p (cc t)")[:, 0:CC * TT],
               qb.ap().rearrange("p cc t -> p (cc t)")[:, 0:CC * TT]]
        bigW = [(d_wk, wk), (d_wv, wv), (d_wr, wr), (d_wf, wf), (d_wo, wo)]
        NQ = len(bigW) * 4

        ACT_QS = [1, 3, 5, 7, 9, 11]
        DVE_QS = [q for q in range(NQ) if q not in ACT_QS]

        def x_dram(i):
            bq, tq = divmod(i, NTS)
            return d_x[bq].rearrange("(cc p) t -> p cc t", p=P)[:, :, tq * TT:(tq + 1) * TT]

        def o_dram(i):
            bq, tq = divmod(i, NTS)
            return d_out[bq].rearrange("(cc p) t -> p cc t", p=P)[:, :, tq * TT:(tq + 1) * TT]

        def gen_sync(b):
            en = "sync"
            for i in range(2):
                b.op(en, "dma_start", out=xb[i % 2][:], in_=x_dram(i), ev=f"ldx_{i}", dma=True)
            for nm, dst, src in [("tmk", c_tmk, d_tmk), ("tmv", c_tmv, d_tmv),
                                 ("tmr", c_tmr, d_tmr), ("td", c_td, d_td),
                                 ("tf", c_tf, d_tf), ("bias", bias, d_bias),
                                 ("ident", ident, d_ident), ("eexp", eexp, d_eexp)]:
                b.op(en, "dma_start", out=dst[:], in_=src, ev=f"ld_{nm}", dma=True)
            b.op(en, "dma_start", out=rt_f[:],
                 in_=d_rt.rearrange("(cc p) e -> p cc e", p=P), ev="ld_rt", dma=True)
            b.op(en, "dma_start",
                 out=ob[:, :, 0:128],
                 in_=d_at.rearrange("(cc p) e -> p cc e", p=P), ev="ld_A", dma=True)
            b.op(en, "dma_start",
                 out=ob[:, :, 128:256],
                 in_=d_bt.rearrange("p (cc e) -> p cc e", cc=CC), ev="ld_B", dma=True)
            for q in range(NQ):
                m, quarter = divmod(q, 4)
                src, _ = bigW[m]
                if q >= 5:
                    b.w(en, f"cast_{q - 5}")
                b.op(en, "dma_start",
                     out=scr[q % 5],
                     in_=src.rearrange("(cc p) d -> p cc d", p=P)[:, quarter * 2:quarter * 2 + 2, :], ev=f"ldq_{q}", dma=True)
            for i in range(2, NT + 2):
                if i < NT:
                    b.w(en, f"out_{i - 2}_{CC - 1}")
                    b.op(en, "dma_start", out=xb[i % 2][:], in_=x_dram(i), ev=f"ldx_{i}", dma=True)
                j = i - 2
                od = o_dram(j)
                for dd in range(CC):
                    b.w(en, f"out_{j}_{dd}")
                    b.op(en, "dma_start", out=od[:, dd:dd + 1, :], in_=ob[:, dd:dd + 1, :],
                         ev=f"st_{j}_{dd}", dma=True)
            for dd in range(CC):
                b.final_wait(en, f"st_{NT - 1}_{dd}")

        def gen_vector(b):
            en = "vector"
            b.op(en, "memset", onecol[:], 1.0 / 1024.0, ev="c_one")
            b.op(en, "memset", ones1[:], 1.0, ev="c_ones1")
            b.op(en, "memset", c_eps[:], EPS, ev="c_eps")
            b.op(en, "memset", c_mhalf[:], -0.5, ev="c_mhalf")
            b.op(en, "memset", c_mone[:], -1.0, ev="c_mone")
            b.w(en, "ld_rt")
            b.op(en, "tensor_copy", rt[:], rt_f[:], ev="cast_rt")
            for nm, dst, src in [("tmk", tmk1, c_tmk), ("tmv", tmv1, c_tmv), ("tmr", tmr1, c_tmr)]:
                b.w(en, f"ld_{nm}")
                b.op(en, "tensor_scalar_add", dst[:], src[:], -1.0, ev=f"c_{nm}")
            b.w(en, "ld_bias")
            b.op(en, "tensor_scalar_mul", negb[:], bias[:], -1.0, ev="c_negb")

            def weight_casts():
                for q in DVE_QS:
                    m, quarter = divmod(q, 4)
                    _, dst = bigW[m]
                    b.w(en, f"ldq_{q}")
                    b.op(en, "tensor_copy",
                         dst[:, quarter * 2:quarter * 2 + 2, :], scr[q % 5], ev=f"cast_{q}")
                b.w(en, "ld_A")
                b.op(en, "tensor_copy", at[:, :, :], ob[:, :, 0:128], ev="cast_A")
                b.w(en, "ld_B")
                b.op(en, "tensor_copy",
                     bt.ap().rearrange("p (cc e) -> p cc e", cc=CC),
                     ob[:, :, 128:256], ev="cast_B")

            def xbf_op(j):
                b.w(en, f"ldx_{j}")
                if j >= 1:
                    b.w(en, f"s12_{j - 1}")
                    b.w(en, f"xsq_{j - 1}")
                b.op(en, "tensor_copy",
                     x_bf.ap().rearrange("p cc t -> p (cc t)"),
                     xb[j % 2].ap().rearrange("p cc t -> p (cc t)"), ev=f"xbf_{j}")

            for i in range(NT):
                if i == 0:
                    xbf_op(0)
                def t2_ops(dds):
                    j2 = i - 1
                    for dd2 in dds:
                        b.w(en, f"wo_{j2}_{dd2}")
                        b.w(en, f"silu_{j2}_{dd2}")
                        b.ws(en, f"num_{j2}_{dd2}")
                        b.op(en, "tensor_tensor",
                             t2[:, dd2, :], ps_s[:, (dd2 % 2) * 256:(dd2 % 2) * 256 + 256],
                             gb[:, dd2, :], ALU.mult, ev=f"t2_{j2}_{dd2}")
                def var_op(j):
                    b.w(en, f"mu2_{j}")
                    if j >= 1:
                        b.w(en, f"rstd_{j - 1}")
                        b.w(en, f"bc_{j - 1}")
                    b.op(en, "tensor_sub", var_r[:], s2_ps, mu2_r[:], ev=f"var_{j}")

                def mur_op(j):
                    b.w(en, f"rstd_{j}")
                    b.op(en, "tensor_mul", mur_r[:], s1_ps, rstd_r[:], ev=f"mur_{j}")

                if i == 0:
                    var_op(0)
                    mur_op(0)
                b.w(en, f"evm_{i}")
                if i >= 1:
                    b.ws(en, f"den_{i - 1}_{CC - 1}")
                b.op(en, "tensor_tensor",
                     ek.ap(), xb[i % 2].ap(),
                     murb_s.ap().unsqueeze(1).broadcast_to([P, CC, TT]),
                     ALU.subtract, ev=f"d1_{i}")
                if i >= 1:
                    t2_ops([0])
                b.ws(en, f"d1_{i}")
                b.w(en, f"evr_{i}")
                if i >= 2:
                    b.w(en, f"wb_{i - 2}_{CC - 1}")
                b.op(en, "tensor_tensor",
                     xn[i % 2][:, :, 2:2 + TT], ek.ap(),
                     rstdb_s.ap().unsqueeze(1).broadcast_to([P, CC, TT]), ALU.mult, ev=f"xnm_{i}")
                if i % NTS == 0:
                    b.op(en, "memset", xn[i % 2][:, :, 1:2], 0.0, ev=f"xnb_{i}")
                else:
                    b.ws(en, f"xnm_{i - 1}")
                    b.op(en, "tensor_copy",
                         xn[i % 2][:, :, 1:2], xn[(i - 1) % 2][:, :, TT + 1:TT + 2], ev=f"xnb_{i}")
                if i >= 1:
                    t2_ops([1])
                b.w(en, f"s12_{i}")
                b.ws(en, f"xnm_{i}")
                b.ws(en, f"xnb_{i}")
                b.op(en, "tensor_tensor",
                     xsq.ap(), xn[i % 2][:, :, 2:2 + TT], xn[i % 2][:, :, 1:1 + TT],
                     ALU.subtract, ev=f"dmix_{i}")
                if i >= 1:
                    t2_ops([2])
                b.ws(en, f"dmix_{i}")
                if i == 0:
                    b.ws(en, "c_tmk")
                    b.ws(en, "c_tmv")
                    b.ws(en, "c_tmr")
                if i >= 1:
                    b.w(en, f"r_{i - 1}_{CC - 1}")
                for cc in range(CC):
                    b.op(en, "scalar_tensor_tensor",
                         xk[:, cc, :], xsq[:, cc, :], tmk1[:, cc:cc + 1],
                         xn[i % 2][:, cc, 2:2 + TT], ALU.mult, ALU.add,
                         ev=f"mixk_{i}" if cc == CC - 1 else None)
                gate_steps = []

                def gstep(k):
                    if k < len(gate_steps):
                        gate_steps[k]()

                def _g0():
                    b.w(en, f"lts_{i}")
                    if i >= 1:
                        b.w(en, f"tg_{i - 1}")
                    b.op(en, "reduce_max", m1_s[:], lt_s.ap().rearrange("p (c e) -> p c e", c=2),
                         axis=AX.X, ev=f"gm1_{i}")

                def _g1():
                    b.ws(en, f"gm1_{i}")
                    b.op(en, "tensor_tensor", msk1_s.ap().rearrange("p (c e) -> p c e", c=2),
                         lt_s.ap().rearrange("p (c e) -> p c e", c=2),
                         m1_s.ap().unsqueeze(2).broadcast_to([P, 2, 8]), ALU.is_equal, ev=f"gmk1_{i}")

                def _g2():
                    b.ws(en, f"gmk1_{i}")
                    b.op(en, "scalar_tensor_tensor", l2_s[:], msk1_s[:], -1e30, lt_s[:],
                         ALU.mult, ALU.add, ev=f"gl2_{i}")

                def _g3():
                    b.ws(en, f"gl2_{i}")
                    b.op(en, "reduce_max", m2_s[:], l2_s.ap().rearrange("p (c e) -> p c e", c=2),
                         axis=AX.X, ev=f"gm2_{i}")

                def _g4():
                    b.ws(en, f"gm2_{i}")
                    b.op(en, "tensor_tensor", msk_s.ap().rearrange("p (c e) -> p c e", c=2),
                         lt_s.ap().rearrange("p (c e) -> p c e", c=2),
                         m2_s.ap().unsqueeze(2).broadcast_to([P, 2, 8]), ALU.is_ge, ev=f"gmsk_{i}")

                def _g5():
                    b.w(en, f"graw_{i}")
                    b.ws(en, f"gmsk_{i}")
                    b.op(en, "tensor_mul", tg_s[:], graw_s[:], msk_s[:], ev=f"gtg_{i}")

                def _g6():
                    b.op(en, "reduce_sum", sall_s[:], graw_s.ap().rearrange("p (c e) -> p c e", c=2),
                         axis=AX.X, ev=f"gsa_{i}")

                def _g7():
                    b.ws(en, f"gtg_{i}")
                    b.op(en, "reduce_sum", st_s[:], tg_s.ap().rearrange("p (c e) -> p c e", c=2),
                         axis=AX.X, ev=f"gss_{i}")

                def _g8():
                    b.ws(en, f"gsa_{i}")
                    b.ws(en, f"gss_{i}")
                    b.op(en, "scalar_tensor_tensor", den2_s[:], sall_s[:], GEPS, st_s[:],
                         ALU.mult, ALU.add, ev=f"gd2_{i}")

                def _g9():
                    b.ws(en, f"gd2_{i}")
                    b.op(en, "reciprocal", rec2_s[:], den2_s[:], ev=f"grc_{i}")

                def _g10():
                    b.ws(en, f"grc_{i}")
                    b.op(en, "tensor_tensor",
                         gate_s.ap().rearrange("p (c e) -> p c e", c=2),
                         tg_s.ap().rearrange("p (c e) -> p c e", c=2),
                         rec2_s.ap().unsqueeze(2).broadcast_to([P, 2, 8]), ALU.mult, ev=f"gsm_{i}")

                gate_steps = [_g0, _g1, _g2, _g3, _g4, _g5, _g6, _g7, _g8, _g9, _g10]
                gk = 0
                for cc in range(CC):
                    b.op(en, "scalar_tensor_tensor",
                         xv[:, cc, :], xsq[:, cc, :], tmv1[:, cc:cc + 1],
                         xn[i % 2][:, cc, 2:2 + TT], ALU.mult, ALU.add,
                         ev=f"mixv_{i}" if cc == CC - 1 else None)
                    gstep(gk); gk += 1
                for cc in range(CC):
                    b.op(en, "scalar_tensor_tensor",
                         xr[:, cc, :], xsq[:, cc, :], tmr1[:, cc:cc + 1],
                         xn[i % 2][:, cc, 2:2 + TT], ALU.mult, ALU.add,
                         ev=f"mixr_{i}" if cc == CC - 1 else None)
                    gstep(gk); gk += 1
                while gk < len(gate_steps):
                    gstep(gk); gk += 1
                if i == 0:
                    weight_casts()
                if i == 0:
                    b.w(en, "c_ew")
                if i >= 1:
                    b.ws(en, f"tts_{i - 1}_{CC - 1}")
                if i % NTS == 0:
                    b.op(en, "memset", pb[:, :, 0:1], 0.0)
                    b.op(en, "memset", qb[:, :, 0:1], 0.0, ev=f"carry_{i}")
                else:
                    b.op(en, "tensor_copy", pb[:, :, 0:1], pb[:, :, TT:TT + 1])
                    b.op(en, "tensor_copy", qb[:, :, 0:1], qb[:, :, TT:TT + 1], ev=f"carry_{i}")
                b.ws(en, f"carry_{i}")
                t2_pre = {0: [3], 1: [5], 3: [7]}
                t2_post = {0: [4], 1: [6]}
                g_sched = {3: 0, 4: 1, 5: 2, 6: 3, 7: 4}

                def g_op(m):
                    b.w(en, f"sbC_{i}_{m}")
                    b.op(en, "scalar_tensor_tensor",
                         gb[:, m, :], ps_base(m),
                         bias[:, m:m + 1], gb[:, m, :],
                         ALU.add, ALU.mult, ev=f"silu_{i}_{m}")

                for dd in range(CC):
                    if i >= 1 and dd in t2_pre:
                        t2_ops(t2_pre[dd])
                    if i >= 1 and dd <= 2:
                        b.w(en, f"out_{i - 1}_{dd}")
                    b.w(en, f"ek_{i}_{dd}")
                    b.w(en, f"v_{i}_{dd}")
                    if i >= 1:
                        b.ws(en, f"t2_{i - 1}_{dd}")
                        if dd > 2:
                            b.w(en, f"out_{i - 1}_{dd}")
                    b.op(en, "tensor_tensor",
                         ekv[:, dd, :], ek[:, dd, :], ps_v[:], ALU.mult, ev=f"ekv_{i}_{dd}")
                    b.ws(en, f"ekv_{i}_{dd}")
                    b.op(en, "tensor_tensor_scan",
                         pb[:, dd, 1:1 + TT], ew[:, dd:dd + 1].broadcast_to([P, TT]),
                         ekv[:, dd, :], pb[:, dd, 0:1], ALU.mult, ALU.add, ev=f"ttp_{i}_{dd}")
                    b.op(en, "tensor_tensor_scan",
                         qb[:, dd, 1:1 + TT], ew[:, dd:dd + 1].broadcast_to([P, TT]),
                         ek[:, dd, :], qb[:, dd, 0:1], ALU.mult, ALU.add, ev=f"tts_{i}_{dd}")
                    if dd == 1:
                        b.w(en, f"h_{i}")
                        b.w(en, f"gB_{i}")
                        if i >= 1:
                            b.w(en, f"wb_{i - 1}_{CC - 1}")
                        b.op(en, "tensor_tensor", ghb[:], ps_h[:, 0:256], gateb_s[:],
                             ALU.mult, ev=f"gh_{i}")
                    if dd == 0:
                        if i == 0:
                            b.w(en, "c_eu")
                        if i >= 1:
                            b.w(en, f"z_{i - 1}_{CC - 2}")
                    b.ws(en, f"ttp_{i}_{dd}")
                    b.op(en, "scalar_tensor_tensor",
                         numb[:, dd, :], ekv[:, dd, :], eu[:, dd:dd + 1],
                         pb[:, dd, 0:TT], ALU.mult, ALU.add, ev=f"num_{i}_{dd}")
                    b.ws(en, f"tts_{i}_{dd}")
                    b.op(en, "scalar_tensor_tensor",
                         denb[:, dd, :], ek[:, dd, :], eu[:, dd:dd + 1],
                         qb[:, dd, 0:TT], ALU.mult, ALU.add, ev=f"den_{i}_{dd}")
                    if i >= 1 and dd in t2_post:
                        t2_ops(t2_post[dd])
                    if dd in g_sched:
                        g_op(g_sched[dd])
                    if dd == 5 and i + 1 < NT:
                        xbf_op(i + 1)
                    if dd == CC - 1:
                        b.w(en, f"oner_{i}_{CC - 1}")
                        b.ws(en, f"den_{i}_{CC - 1}")
                        b.op(en, "tensor_tensor",
                             ek[:, CC - 1, :], denb[:, CC - 1, :], erb[:, CC - 1, :],
                             ALU.mult, ev=f"denp_{i}_{CC - 1}")
                        b.ws(en, f"denp_{i}_{CC - 1}")
                        b.op(en, "reciprocal",
                             denb[:, CC - 1, :], ek[:, CC - 1, :], ev=f"recd_{i}")
                        b.ws(en, f"recd_{i}")
                        if i >= 1:
                            b.w(en, f"wo_{i - 1}_{CC - 1}")
                        b.op(en, "tensor_tensor",
                             zb[:, CC - 1, :], numb[:, CC - 1, :], denb[:, CC - 1, :],
                             ALU.mult, ev=f"y_{i}")
                g_op(5)
                if i + 1 < NT:
                    var_op(i + 1)
                g_op(6)
                if i + 1 < NT:
                    mur_op(i + 1)
                g_op(7)
            j = NT - 1
            for dd in range(CC):
                b.w(en, f"wo_{j}_{dd}")
                b.w(en, f"silu_{j}_{dd}")
                b.ws(en, f"num_{j}_{dd}")
                b.op(en, "tensor_tensor",
                     t2[:, dd, :], ps_s[:, (dd % 2) * 256:(dd % 2) * 256 + 256],
                     gb[:, dd, :], ALU.mult, ev=f"t2_{j}_{dd}")

        def gen_scalar(b):
            en = "scalar"
            b.w(en, "ld_td")
            b.w(en, "c_mone")
            b.w(en, "c_mhalf")
            b.w(en, "c_eps")
            b.op(en, "activation", ew[:], c_td[:], AF.Exp, ev="ewt")
            b.ws(en, "ewt")
            b.op(en, "activation", ew[:], ew[:], AF.Exp, scale=c_mone[:], ev="c_ew")
            b.w(en, "ld_tf")
            b.op(en, "activation", eu[:], c_tf[:], AF.Exp, ev="c_eu")

            for i in range(NT):
                def xsq_op(j):
                    b.w(en, f"xbf_{j}")
                    if j >= 1:
                        b.w(en, f"mixr_{j - 1}")
                    b.op(en, "activation",
                         xsq.ap().rearrange("p cc t -> p (cc t)"),
                         x_bf.ap().rearrange("p cc t -> p (cc t)"), AF.Square, ev=f"xsq_{j}")

                def mu2_op(j):
                    b.w(en, f"s12_{j}")
                    if j >= 1:
                        b.w(en, f"var_{j - 1}")
                    b.op(en, "activation", mu2_r[:], s1_ps, AF.Square, ev=f"mu2_{j}")

                def ln_op(j):
                    b.w(en, f"var_{j}")
                    if j >= 1:
                        b.w(en, f"bc_{j - 1}")
                    b.op(en, "activation", rstd_r[:], var_r[:], AF.Ln,
                         bias=c_eps[0:1, :], ev=f"ln_{j}")

                def rstd_op(j):
                    b.ws(en, f"ln_{j}")
                    b.op(en, "activation", rstd_r[:], rstd_r[:], AF.Exp,
                         scale=c_mhalf[0:1, :], ev=f"rstd_{j}")

                def ev_ops(j):
                    b.w(en, f"bc_{j}")
                    if j >= 1:
                        b.w(en, f"xnm_{j - 1}")
                        b.w(en, f"d1_{j - 1}")
                    b.op(en, "copy", rstdb_s[:], ps_bc[:, 0:256], ev=f"evr_{j}")
                    b.op(en, "copy", murb_s[:], ps_bc[:, 256:512], ev=f"evm_{j}")

                if i == 0:
                    xsq_op(0)
                    mu2_op(0)
                    ln_op(0)
                    rstd_op(0)
                ev_ops(i)
                b.w(en, f"lt_{i}")
                if i >= 1:
                    b.w(en, f"gsm_{i - 1}")
                b.op(en, "copy", lt_s[:], ps_h[:, 256:272], ev=f"lts_{i}")
                b.ws(en, f"lts_{i}")
                b.op(en, "activation", graw_s[:], lt_s[:], AF.Exp, ev=f"graw_{i}")
                if i == 0:
                    for q in ACT_QS:
                        m0, quarter = divmod(q, 4)
                        _, dst = bigW[m0]
                        b.w(en, f"ldq_{q}")
                        b.op(en, "copy",
                             dst[:, quarter * 2:quarter * 2 + 2, :], scr[q % 5],
                             ev=f"cast_{q}")
                def silu_op(m):
                    half = ps_base(m)
                    b.w(en, f"wb_{i}_{m}")
                    if i == 0 and m == 0:
                        b.w(en, "c_negb")
                    if m == CC - 1:
                        b.w(en, f"y_{i}")
                    else:
                        b.w(en, f"z_{i}_{m}")
                    b.op(en, "activation", gb[:, m, :], half, AF.Exp,
                         scale=c_mone[:], bias=negb[:, m:m + 1], ev=f"sbA_{i}_{m}")
                    b.ws(en, f"sbA_{i}_{m}")
                    b.op(en, "activation", gb[:, m, :], gb[:, m, :], AF.Ln,
                         bias=1.0, ev=f"sbB_{i}_{m}")
                    b.ws(en, f"sbB_{i}_{m}")
                    b.op(en, "activation", gb[:, m, :], gb[:, m, :], AF.Exp,
                         scale=c_mone[:], ev=f"sbC_{i}_{m}")

                def rec_op(cc):
                    b.w(en, f"denp_{i}_{cc}")
                    b.op(en, "activation", denb[:, cc, :], denb[:, cc, :], AF.Ln, ev=f"lnr_{i}_{cc}")
                    b.ws(en, f"lnr_{i}_{cc}")
                    b.op(en, "activation",
                         denb[:, cc, :], denb[:, cc, :], AF.Exp, scale=c_mone[:], ev=f"rec_{i}_{cc}")

                for dd in range(CC):
                    b.w(en, f"k_{i}_{dd}")
                    if dd == 0:
                        b.w(en, f"xnm_{i}")
                    b.op(en, "activation",
                         ek[:, dd, :], ps_k[:], AF.Exp, ev=f"ek_{i}_{dd}")
                    b.w(en, f"r_{i}_{dd}")
                    if i >= 1 and dd == 0:
                        b.w(en, f"denp_{i - 1}_{CC - 1}")
                    b.op(en, "activation",
                         erb[:, dd, :], ps_r[:], AF.Exp, scale=c_mone[:], ev=f"sr_{i}_{dd}")
                    if dd == 1:
                        b.w(en, f"tg_{i}")
                        b.op(en, "copy", gT_s[:], ps_g[0:8, 256:512], ev=f"gT_{i}")
                        b.w(en, f"ee_{i}")
                        if i >= 1:
                            b.w(en, f"gh_{i - 1}")
                        b.op(en, "copy", gateb_s[:], ps_g[:, 0:256], ev=f"gB_{i}")
                    if dd >= 1:
                        rec_op(dd - 1)
                    if dd == 6 and i + 1 < NT:
                        xsq_op(i + 1)
                    if dd >= 3:
                        silu_op(dd - 3)
                silu_op(5)
                if i + 1 < NT:
                    mu2_op(i + 1)
                silu_op(6)
                if i + 1 < NT:
                    ln_op(i + 1)
                    rstd_op(i + 1)
                silu_op(7)

        def gen_tensor(b):
            en = "tensor"
            b.w(en, "c_one")
            b.w(en, "c_ones1")

            def wo_grp(j, dd):
                if dd == 0:
                    if j == 0:
                        b.w(en, f"cast_{NQ - 1}")
                    elif j >= 1:
                        b.w(en, f"t2_{j - 1}_{CC - 1}")
                    b.w(en, f"y_{j}")
                else:
                    b.w(en, f"t2_{j}_{dd - 1}")
                for cc in range(CC):
                    b.op(en, "matmul",
                         ps_s[:, (dd % 2) * 256:(dd % 2) * 256 + 256],
                         wo[:, cc, dd * 128:(dd + 1) * 128], zb[:, cc, :],
                         start=(cc == 0), stop=(cc == CC - 1),
                         ev=f"wo_{j}_{dd}" if cc == CC - 1 else None)

            def base_grp(i, dd):
                if dd <= 1:
                    if i == 0:
                        b.w(en, "cast_15")
                        b.w(en, "cast_B")
                    else:
                        b.w(en, f"silu_{i - 1}_{CC - 2 + dd}")
                else:
                    b.w(en, f"silu_{i}_{dd - 2}")
                for cc in range(CC):
                    b.op(en, "matmul",
                         ps_base(dd),
                         wf[:, cc, dd * 128:(dd + 1) * 128],
                         xn[i % 2][:, cc, 2:2 + TT], start=(cc == 0), stop=False)
                if dd == 0:
                    b.w(en, f"gh_{i}")
                b.op(en, "matmul",
                     ps_base(dd),
                     bt[:, dd * 128:(dd + 1) * 128], ghb[:], start=False, stop=True,
                     ev=f"wb_{i}_{dd}")

            def s12_grp(j):
                b.w(en, f"xbf_{j}")
                if j >= 1:
                    b.w(en, f"mu2_{j - 1}")
                    b.w(en, f"mur_{j - 1}")
                    b.w(en, f"var_{j - 1}")
                    b.w(en, f"gT_{j - 1}")
                    b.w(en, f"gB_{j - 1}")
                for cc in range(CC):
                    b.op(en, "matmul", s1_ps, onecol[:], x_bf[:, cc, :],
                         start=(cc == 0), stop=(cc == CC - 1))
                b.w(en, f"xsq_{j}")
                for cc in range(CC):
                    b.op(en, "matmul", s2_ps, onecol[:], xsq[:, cc, :],
                         start=(cc == 0), stop=(cc == CC - 1),
                         ev=f"s12_{j}" if cc == CC - 1 else None)

            def bc_grp(j):
                b.w(en, f"rstd_{j}")
                b.w(en, f"mur_{j}")
                b.w(en, f"mu2_{j}")
                b.op(en, "matmul", ps_bc[:, 0:256], ones1[:], rstd_r[:], start=True, stop=True)
                b.op(en, "matmul", ps_bc[:, 256:512], ones1[:], mur_r[:],
                     start=True, stop=True, ev=f"bc_{j}")

            for i in range(NT):
                j = i - 1
                if i == 0:
                    s12_grp(0)
                bc_grp(i)
                if i >= 1:
                    wo_grp(j, 0)
                    wo_grp(j, 1)
                    wo_grp(j, 2)
                b.w(en, f"xnm_{i}")
                if i == 0:
                    b.w(en, "cast_rt")
                if i >= 1:
                    b.w(en, f"lts_{i - 1}")
                    b.w(en, f"gh_{i - 1}")
                for tk in range(2):
                    for cc in range(CC):
                        b.op(en, "matmul",
                             ps_h[:, 256 + tk * 8:256 + tk * 8 + 8],
                             xn[i % 2][:, cc, 2 + tk * 128:2 + tk * 128 + 128],
                             rt[:, cc, :], start=(cc == 0), stop=(cc == CC - 1),
                             ev=f"lt_{i}" if (tk == 1 and cc == CC - 1) else None)
                wo_sched = {0: 3, 1: 5, 2: 6, 3: 7}
                for dd in range(CC):
                    if i >= 1 and dd in wo_sched:
                        wo_grp(j, wo_sched[dd])
                    if dd == 0:
                        b.w(en, f"mixk_{i}")
                        if i == 0:
                            b.w(en, "cast_2")
                            b.w(en, "cast_3")
                    if dd >= 1:
                        b.w(en, f"ek_{i}_{dd - 1}")
                    elif i >= 1:
                        b.w(en, f"ek_{i - 1}_{CC - 1}")
                    for cc in range(CC):
                        b.op(en, "matmul",
                             ps_k[:], wk[:, cc, dd * 128:(dd + 1) * 128],
                             xk[:, cc, :], start=(cc == 0), stop=(cc == CC - 1),
                             ev=f"k_{i}_{dd}" if cc == CC - 1 else None)
                    if dd == 0:
                        b.w(en, f"mixv_{i}")
                        if i == 0:
                            b.w(en, "cast_6")
                            b.w(en, "cast_7")
                    if dd >= 1:
                        b.w(en, f"ekv_{i}_{dd - 1}")
                    elif i >= 1:
                        b.w(en, f"ekv_{i - 1}_{CC - 1}")
                    for cc in range(CC):
                        b.op(en, "matmul",
                             ps_v[:], wv[:, cc, dd * 128:(dd + 1) * 128],
                             xv[:, cc, :], start=(cc == 0), stop=(cc == CC - 1),
                             ev=f"v_{i}_{dd}" if cc == CC - 1 else None)
                    if dd == 0:
                        b.w(en, f"mixr_{i}")
                        if i == 0:
                            b.w(en, "cast_10")
                            b.w(en, "cast_11")
                    if dd >= 1:
                        b.w(en, f"sr_{i}_{dd - 1}")
                    elif i >= 1:
                        b.w(en, f"sr_{i - 1}_{CC - 1}")
                    for cc in range(CC):
                        b.op(en, "matmul",
                             ps_r[:], wr[:, cc, dd * 128:(dd + 1) * 128],
                             xr[:, cc, :], start=(cc == 0), stop=(cc == CC - 1),
                             ev=f"r_{i}_{dd}" if cc == CC - 1 else None)
                    if dd == 0:
                        if i >= 1:
                            wo_grp(j, 4)
                    elif dd == 1:
                        if i == 0:
                            b.w(en, "cast_A")
                        b.w(en, f"lts_{i}")
                        for cc in range(CC):
                            b.op(en, "matmul",
                                 ps_h[:, 0:256], at[:, cc, :], xn[i % 2][:, cc, 2:2 + TT],
                                 start=(cc == 0), stop=(cc == CC - 1),
                                 ev=f"h_{i}" if cc == CC - 1 else None)
                        b.w(en, f"gsm_{i}")
                        b.w(en, f"evm_{i}")
                        b.w(en, f"evr_{i}")
                        if i == 0:
                            b.w(en, "ld_ident")
                            b.w(en, "ld_eexp")
                        if i >= 1:
                            b.w(en, f"gB_{i - 1}")
                        for tk in range(2):
                            b.op(en, "transpose",
                                 ps_g[0:8, 256 + tk * 128:256 + tk * 128 + 128],
                                 gate_s[:, tk * 8:tk * 8 + 8], ident[:],
                                 ev=f"tg_{i}" if tk == 1 else None)
                        b.w(en, f"gT_{i}")
                        b.op(en, "matmul", ps_g[:, 0:256], eexp[:], gT_s[:],
                             start=True, stop=True, ev=f"ee_{i}")
                    else:
                        base_grp(i, dd - 2)
                    if dd == 7 and i + 1 < NT:
                        s12_grp(i + 1)
                base_grp(i, CC - 2)
                base_grp(i, CC - 1)
            for dd in range(CC):
                wo_grp(NT - 1, dd)

        def gen_gpsimd(b):
            en = "gpsimd"
            b.op(en, "memset", onesb[:], 1.0, ev="c_onesb")
            for i in range(NT):
                if i >= 1:
                    if i == 1:
                        b.w(en, "cast_A")
                        b.w(en, "cast_B")
                    for dd in range(3):
                        b.w(en, f"t2_{i - 1}_{dd}")
                        if i >= 2:
                            b.w(en, f"st_{i - 2}_{dd}")
                        b.op(en, "tensor_tensor",
                             ob[:, dd, :], t2[:, dd, :], xb[(i - 1) % 2][:, dd, :],
                             ALU.add, ev=f"out_{i - 1}_{dd}")
                out_sched = {0: [3, 4], 1: [5, 6], 2: [7]}
                if i == 0:
                    b.ws(en, "c_onesb")
                for cc in range(CC):
                    b.w(en, f"sr_{i}_{cc}")
                    if i >= 1 and cc == 0:
                        b.w(en, f"denp_{i - 1}_{CC - 1}")
                    b.op(en, "tensor_tensor",
                         erb[:, cc, :], erb[:, cc, :], onesb[:], ALU.add, ev=f"oner_{i}_{cc}")
                    if cc < CC - 1:
                        b.w(en, f"den_{i}_{cc}")
                        b.ws(en, f"oner_{i}_{cc}")
                        b.op(en, "tensor_tensor",
                             denb[:, cc, :], denb[:, cc, :], erb[:, cc, :],
                             ALU.mult, ev=f"denp_{i}_{cc}")
                    if i >= 1 and cc in out_sched:
                        for dd in out_sched[cc]:
                            b.w(en, f"t2_{i - 1}_{dd}")
                            if i >= 2:
                                b.w(en, f"st_{i - 2}_{dd}")
                            b.op(en, "tensor_tensor",
                                 ob[:, dd, :], t2[:, dd, :], xb[(i - 1) % 2][:, dd, :],
                                 ALU.add, ev=f"out_{i - 1}_{dd}")
                    if cc >= 1:
                        b.w(en, f"rec_{i}_{cc - 1}")
                        b.w(en, f"num_{i}_{cc - 1}")
                        if cc == 1 and i >= 1:
                            b.w(en, f"wo_{i - 1}_{CC - 1}")
                        b.op(en, "tensor_tensor",
                             zb[:, cc - 1, :], numb[:, cc - 1, :], denb[:, cc - 1, :],
                             ALU.mult, ev=f"z_{i}_{cc - 1}")
            i = NT
            for dd in range(CC):
                b.w(en, f"t2_{i - 1}_{dd}")
                b.w(en, f"st_{i - 2}_{dd}")
                b.op(en, "tensor_tensor",
                     ob[:, dd, :], t2[:, dd, :], xb[(i - 1) % 2][:, dd, :],
                     ALU.add, ev=f"out_{i - 1}_{dd}")

        def gen_all(b):
            gen_sync(b)
            gen_vector(b)
            gen_scalar(b)
            gen_tensor(b)
            gen_gpsimd(b)

        gen_all(b)
        b.start_emit()
        with ExitStack() as semstack:
            def mksem(name):
                return semstack.enter_context(nc.semaphore(name))
            b.sem = {"sync": mksem("sm_sync"), "vector": mksem("sm_vec"),
                     "scalar": mksem("sm_sca"), "tensor": mksem("sm_ten"),
                     "gpsimd": mksem("sm_gps")}
            for dj in range(Bld.NDMA):
                b.sem[f"dma{dj}"] = mksem(f"sm_dma{dj}")
            block = semstack.enter_context(nc.Block())

            @block.sync
            def _(sync):
                b.eng["sync"] = sync
                gen_sync(b)

            @block.vector
            def _(vector):
                b.eng["vector"] = vector
                gen_vector(b)

            @block.scalar
            def _(scalar):
                b.eng["scalar"] = scalar
                gen_scalar(b)

            @block.tensor
            def _(tensor):
                b.eng["tensor"] = tensor
                gen_tensor(b)

            @block.gpsimd
            def _(gpsimd):
                b.eng["gpsimd"] = gpsimd
                gen_gpsimd(b)

    return nc


def kernel(L0_attn_x, L0_attn_time_decay, L0_attn_time_first,
           L0_attn_time_mix_k, L0_attn_time_mix_v, L0_attn_time_mix_r,
           L0_attn_W_key, L0_attn_W_value, L0_attn_W_receptance,
           L0_attn_W_output, L0_ffn_W, L0_ffn_b, L0_ffn_R, L0_ffn_A,
           L0_ffn_B):
    x = np.asarray(L0_attn_x)
    assert x.shape == (16, 2048, 1024)

    def f32c(a):
        return np.ascontiguousarray(np.asarray(a), dtype=np.float32)

    x_t = np.ascontiguousarray(x.transpose(0, 2, 1))
    wkT = f32c(np.asarray(L0_attn_W_key).T)
    wvT = f32c(np.asarray(L0_attn_W_value).T)
    wrT = f32c(np.asarray(L0_attn_W_receptance).T)
    woT = f32c(np.asarray(L0_attn_W_output).T)
    wfT = f32c(np.asarray(L0_ffn_W).T)
    aT = f32c(np.asarray(L0_ffn_A).reshape(128, C).T)
    bT = f32c(np.asarray(L0_ffn_B).transpose(0, 2, 1).reshape(128, C))
    rT = f32c(np.asarray(L0_ffn_R).T)

    def pc(a):
        return np.ascontiguousarray(np.asarray(a, dtype=np.float32).reshape(CC, P).T)

    tmk = pc(np.asarray(L0_attn_time_mix_k).reshape(-1))
    tmv = pc(np.asarray(L0_attn_time_mix_v).reshape(-1))
    tmr = pc(np.asarray(L0_attn_time_mix_r).reshape(-1))
    td = pc(L0_attn_time_decay)
    tf = pc(L0_attn_time_first)
    bias = pc(L0_ffn_b)
    ident = np.eye(P, dtype=np.float32)
    eexp = np.zeros((8, 128), dtype=np.float32)
    for e in range(8):
        eexp[e, e * 16:(e + 1) * 16] = 1.0

    nc = build_nc()
    in_maps = []
    for core in range(8):
        in_maps.append({
            "xt": np.ascontiguousarray(x_t[core * 2:core * 2 + 2]),
            "wkT": wkT, "wvT": wvT, "wrT": wrT, "woT": woT, "wfT": wfT,
            "aT": aT, "bT": bT, "rT": rT,
            "tmk": tmk, "tmv": tmv, "tmr": tmr, "td": td, "tf": tf,
            "bias": bias, "ident": ident, "eexp": eexp,
        })
    res = run_bass_kernel_spmd(nc, in_maps, core_ids=list(range(8)))
    out_t = np.concatenate([r["out_t"] for r in res.results], axis=0)
    return np.ascontiguousarray(out_t.transpose(0, 2, 1))


